# version 07
# baseline (speedup 1.0000x reference)
import numpy as np
import ml_dtypes
from contextlib import ExitStack
import concourse.bass as bass
import concourse.mybir as mybir

F32 = mybir.dt.float32
BF16 = mybir.dt.bfloat16
Alu = mybir.AluOpType
Act = mybir.ActivationFunctionType
AX = mybir.AxisListType.X

M1 = 12582912.0
M2 = 6755399441055744.0
M3 = 3221225472.0

NSEQ = 2
L = 2048
Q = 128
GRP = 512
NGRP = L // GRP
NCH = GRP // Q
H = 64
DST = 16
DM = 1024
DI = 2048
CONV = 2080
EPS = 1e-5
NEG = -1e9

ENGINES = ("sync", "scalar", "vector", "tensor", "gpsimd")


class Ctx:
    def __init__(self, nc, stack):
        self.nc = nc
        self.stack = stack
        self.ops = {k: [] for k in ENGINES}
        self.sems = {}
        self.cnt = {}
        self._sync_cls = None

    def _switch_cls(self, cls):
        if cls != self._sync_cls and self._sync_cls is not None:
            self.wait("sync", "tr")
            self.wait("sync", "st")
        self._sync_cls = cls

    def sem(self, name):
        if name not in self.sems:
            self.sems[name] = self.stack.enter_context(self.nc.semaphore(name))
            self.cnt[name] = 0
        return self.sems[name]

    def emit(self, eng, fn):
        self.ops[eng].append(fn)

    def op(self, eng, fn, inc=None, inck=1):
        if inc is not None:
            s = self.sem(inc)
            self.cnt[inc] += inck
            self.emit(eng, lambda e, fn=fn, s=s, k=inck: fn(e).then_inc(s, k))
        else:
            self.emit(eng, lambda e, fn=fn: fn(e))

    def dma(self, eng, out, in_, inc, s2s=False, **kw):
        if eng == "sync":
            self._switch_cls("s2s" if s2s else "other")
        s = self.sem(inc)
        self.cnt[inc] += 16
        self.emit(eng, lambda e, out=out, in_=in_, s=s, kw=kw: e.dma_start(out, in_, **kw).then_inc(s, 16))

    def dmat(self, eng, out, in_, inc):
        self._switch_cls("tr")
        s = self.sem(inc)
        self.cnt[inc] += 16
        self.emit(eng, lambda e, out=out, in_=in_, s=s: e.dma_start_transpose(out, in_).then_inc(s, 16))

    def wait(self, eng, sem, val=None):
        s = self.sem(sem)
        v = self.cnt[sem] if val is None else val
        if v > 0:
            self.emit(eng, lambda e, s=s, v=v: e.wait_ge(s, v))

    def mark(self, sem):
        return (sem, self.cnt[sem])

    def wait_mark(self, eng, mark):
        sem, v = mark
        self.wait(eng, sem, v)

    def barrier(self):
        for eng in ENGINES:
            for sname in list(self.cnt):
                self.wait(eng, sname)

    def finalize(self):
        nc = self.nc
        with nc.Block() as block:
            @block.sync
            def _(e):
                for f in self.ops["sync"]:
                    f(e)

            @block.scalar
            def _(e):
                for f in self.ops["scalar"]:
                    f(e)

            @block.vector
            def _(e):
                for f in self.ops["vector"]:
                    f(e)

            @block.tensor
            def _(e):
                for f in self.ops["tensor"]:
                    f(e)

            @block.gpsimd
            def _(e):
                for f in self.ops["gpsimd"]:
                    f(e)


def build(dbg=(), phases=("ssd", "wout", "B"), ssd_parts=("stage", "g", "r", "states", "ey", "gate", "rms")):
    nc = bass.Bass("TRN2", target_bir_lowering=False, detect_race_conditions=False)
    stack = ExitStack()
    cx = Ctx(nc, stack)
    dbg = set(dbg)

    dr = {}
    def din(name, shape, dt=F32):
        dr[name] = nc.dram_tensor(name, list(shape), dt, kind="ExternalInput").ap()
        return dr[name]

    u = din("u", [NSEQ, L, DM])
    din("wint", [33, 8, 128, 128])
    din("woutt", [2048, 1024])
    din("convw", [128, 17 * 4])
    din("convb", [128, 17])
    din("dtbias", [128, 1])
    din("alog", [128, 1])
    din("dvec", [128, 16])
    din("normw", [128, 16])
    din("identb", [128, 128], BF16)
    din("identf", [128, 128], F32)
    din("maskt", [128, 128], BF16)
    din("masktf", [128, 128], F32)
    din("onesf", [128, 128], F32)
    din("onesb", [1, 128], BF16)
    din("delta3", [4, 128], BF16)
    din("onescolb", [128, 1], BF16)
    din("onesallb", [128, 128], BF16)
    din("mask4b", [128, 512], BF16)
    din("wgt", [8, 128, 4096])
    din("wut", [8, 128, 4096])
    din("wdt", [32, 128, 1024])

    out = nc.dram_tensor("out", [NSEQ, L, DM], F32, kind="ExternalOutput").ap()
    outacc = nc.dram_tensor("outacc", [NSEQ, L, DM], F32, kind="Internal").ap()
    xq2scr = nc.dram_tensor("xq2scr", [32, 128, 32 * 128], BF16, kind="Internal").ap()
    gscr = nc.dram_tensor("gscr", [32, 128, 4096], BF16, kind="Internal").ap()
    xq1scr = nc.dram_tensor("xq1scr", [32, 128, 8 * 128], BF16, kind="Internal").ap()

    taps = {}
    def tap(name, shape, dt=F32):
        if name in dbg:
            taps[name] = nc.dram_tensor("tap_" + name, list(shape), dt, kind="ExternalOutput").ap()
            return taps[name]
        return None

    ARENA_SZ = 212800
    backing = stack.enter_context(nc.sbuf_tensor("arena", [128, ARENA_SZ], mybir.dt.uint8))
    backing_addr = nc.lookup_mloc(backing).addr
    zone_off = {"P": 0, "A": 40960, "B": 40960}
    zone_cap = {"P": 40960, "A": ARENA_SZ, "B": ARENA_SZ}

    def sb(name, shape, dt, zone="A"):
        import functools, operator
        per_part = functools.reduce(operator.mul, list(shape)[1:], 1) * mybir.dt.size(dt)
        addr = zone_off[zone]
        zone_off[zone] = (addr + per_part + 31) // 32 * 32
        assert zone_off[zone] <= zone_cap[zone], f"zone {zone} overflow at {name}: {zone_off[zone]}"
        h = nc.virtual_sbuf_tensor(name, list(shape), dt)
        nc._tensor(h.name, h.shape, h.dtype, type="SB")
        mloc = nc.lookup_mloc(h)
        mloc.allocated = True
        mloc.addr = backing_addr + addr
        return h

    def pt(name, shape, dt=F32):
        return stack.enter_context(nc.psum_tensor(name, list(shape), dt))

    c_identb = sb("c_identb", [128, 128], BF16, zone="P")
    c_identf = sb("c_identf", [128, 128], F32, zone="P")
    c_maskt = sb("c_maskt", [128, 128], BF16, zone="P")
    c_masktf = sb("c_masktf", [128, 128], F32, zone="P")
    c_onesf = sb("c_onesf", [128, 128], F32, zone="P")
    c_ones = sb("c_ones", [1, 128], BF16, zone="P")
    c_delta3 = sb("c_delta3", [4, 128], BF16, zone="P")
    c_onescolb = sb("c_onescolb", [128, 1], BF16, zone="P")
    c_onesallb = sb("c_onesallb", [128, 128], BF16, zone="P")
    c_mask4 = sb("c_mask4", [128, 512], BF16, zone="P")
    c_convw = sb("c_convw", [128, 68], F32, zone="P")
    c_convb = sb("c_convb", [128, 17], F32, zone="P")
    c_dtb = sb("c_dtb", [128, 1], F32, zone="P")
    c_A = sb("c_A", [128, 1], F32, zone="P")
    c_dvec = sb("c_dvec", [128, 16], F32, zone="P")
    c_normw = sb("c_normw", [128, 16], F32, zone="P")
    convdiag = sb("convdiag", [128, 8 * 128], BF16)

    winbuf = sb("winbuf", [128, 2 * 8 * 128], BF16)
    wobuf = sb("wobuf", [128, 2 * 512], BF16, zone="P")

    uc = sb("uc", [128, 2 * DM], F32, zone="P")
    ures = sb("ures", [128, DM], F32, zone="P")
    xnb = sb("xnb", [128, 2 * DM], BF16)
    xnF = sb("xnF", [128, 8 * GRP], BF16)
    stat = sb("stat", [128, 16], F32, zone="P")
    sq = sb("sq", [128, DM], BF16, zone="P")
    siluzF = sb("siluzF", [128, 16 * GRP], BF16)
    xbcF = sb("xbcF", [128, 17 * (GRP + 4)], BF16)
    convxF = sb("convxF", [128, 16 * GRP], BF16)
    bct = sb("bct", [48, GRP], BF16)
    ctile0 = sb("ctile0", [16, GRP], BF16)
    dtF = sb("dtF", [128, GRP], F32)
    abar = sb("abar", [128, GRP], F32)
    acum = sb("acum", [128, GRP], F32)
    decayF = sb("decayF", [128, Q], F32)
    dtT = sb("dtT", [128, 64], BF16)
    decayT = sb("decayT", [128, 64], BF16)
    negAcumT = sb("negAcumT", [128, 64], F32)
    aflatH = sb("aflatH", [128, 22 * Q], BF16)
    aflatL = sb("aflatL", [128, 22 * Q], BF16)
    aflatL2 = sb("aflatL2", [128, 22 * Q], BF16)
    ahib = sb("ahib", [128, Q], BF16)
    alob = sb("alob", [128, Q], BF16)
    alo2b = sb("alo2b", [128, Q], BF16)
    tA = sb("tA", [128, Q], F32)
    expacumF = sb("expacumF", [128, Q], BF16)
    decendcol = sb("decendcol", [128, 1], F32)
    decrow = sb("decrow", [1, 64], F32)
    expR3 = sb("expR3", [4, 22 * Q], BF16)
    ccrep = sb("ccrep", [128, Q], BF16)
    x_T = sb("x_T", [128, DI], BF16)
    xdt_T = sb("xdt_T", [128, DI], BF16)
    xdd_T = sb("xdd_T", [128, DI], BF16)
    bt_tok = sb("bt_tok", [128, 16], BF16)
    etile = sb("etile", [128, 4 * 128], BF16)
    emtile = sb("emtile", [128, 4 * 128], BF16)
    stile = sb("stile", [128, 128], BF16)
    r3 = sb("r3", [128, 22 * Q], BF16)
    prevR = sb("prevR", [128, 22 * 32], BF16)
    prevb = sb("prevb", [16, DI], BF16)
    prev0 = sb("prev0", [16, DI], F32)
    decend = sb("decend", [16, 64], F32)
    yP = sb("yP", [32, 16 * Q], BF16)
    yF = sb("yF", [128, 16 * Q], BF16)
    yg = sb("yg", [128, 16 * Q], BF16)
    ygw = sb("ygw", [128, 16 * GRP], BF16)
    invrg = sb("invrg", [128, NCH], F32, zone="P")
    ygsq = sb("ygsq", [128, 16 * Q], BF16)
    rmsrow = sb("rmsrow", [1, Q], F32, zone="P")
    att = sb("att", [128, DM], F32, zone="P")

    xnt = sb("xnt", [128, DM], F32, zone="B")
    qv = sb("qv", [128, 2048], F32, zone="B")
    qa = sb("qa", [128, 2048], F32, zone="B")
    qb = sb("qb", [128, 2048], F32, zone="B")
    qc = sb("qc", [128, 2048], F32, zone="B")
    xq1t = sb("xq1t", [128, DM], BF16, zone="B")
    xq1F = sb("xq1F", [128, 8 * 128], BF16, zone="B")
    gateT = sb("gateT", [128, 4096], BF16, zone="B")
    hpT = sb("hpT", [128, 4096], F32, zone="B")
    xq2t = sb("xq2t", [128, 4096], BF16, zone="B")
    stripb = sb("stripb", [128, 4096], BF16, zone="B")
    wslot = sb("wslot", [128, 32768], BF16, zone="B")
    wtmp = sb("wtmp", [128, 4096], F32, zone="B")
    mwc = sb("mwc", [128, 8], F32, zone="P")
    mws = sb("mws", [128, 4], F32, zone="P")
    mwrow = sb("mwrow", [1, 2], F32, zone="P")
    vblend = sb("vblend", [128, 1], F32, zone="P")

    pb = [pt("pb%d" % i, [128, GRP]) for i in range(8)]
    ps_win = [pb[0], pb[1]]
    ps_conv = [pb[2], pb[3]]

    for name, t in [("identb", c_identb), ("identf", c_identf), ("maskt", c_maskt), ("masktf", c_masktf), ("onesf", c_onesf),
                    ("onesb", c_ones), ("delta3", c_delta3), ("onescolb", c_onescolb), ("onesallb", c_onesallb), ("mask4b", c_mask4), ("convw", c_convw),
                    ("convb", c_convb), ("dtbias", c_dtb), ("alog", c_A),
                    ("dvec", c_dvec), ("normw", c_normw)]:
        cx.dma("scalar", t.ap(), dr[name], "kld")
    cx.wait("vector", "kld")
    cx.wait("scalar", "kld")
    cx.op("scalar", lambda e: e.activation(c_A[64:128, :], c_A[64:128, :], Act.Exp), inc="sc")
    cx.wait("vector", "sc")
    cx.op("vector", lambda e: e.tensor_scalar_mul(c_A[64:128, :], c_A[64:128, :], -1.0), inc="vc")
    cx.op("vector", lambda e: e.memset(vblend[:, :], 1.0), inc="vc")
    cx.wait("vector", "vc")
    cx.op("vector", lambda e: e.tensor_scalar(vblend[:, :], vblend[:, :], 9.094947017729282e-13, None, Alu.add), inc="vc")
    cx.wait("vector", "vc")
    cx.op("vector", lambda e: e.tensor_scalar(vblend[:, :], vblend[:, :], 1.0, 1099511627776.0, Alu.subtract, Alu.mult), inc="vc")
    cx.wait("vector", "vc")
    cx.op("vector", lambda e: e.tensor_scalar(vblend[:, :], vblend[:, :], -1.0, 1.0, Alu.mult, Alu.add), inc="vc")
    cx.wait("vector", "vc")
    cx.op("vector", lambda e: e.memset(expR3[0:4, :], 0.0), inc="vc")
    cx.op("vector", lambda e: e.memset(ccrep[:, :], 0.0), inc="vc")
    cx.op("vector", lambda e: e.memset(prevR[:, :], 0.0), inc="vc")
    cx.op("vector", lambda e: e.memset(xbcF[:, 16 * (GRP + 4): 17 * (GRP + 4)], 0.0), inc="vc")
    cx.barrier()

    tap_xn = tap("xn", [NSEQ, L, DM], BF16)
    tap_xbcpre = tap("xbcpre", [NSEQ, L, CONV], BF16)
    tap_siluz = tap("siluz", [NSEQ, L, DI], BF16)
    tap_dt = tap("dt", [NSEQ, H, L])
    tap_acum = tap("acum", [NSEQ, 16, H, Q])
    tap_xbc = tap("xbc", [NSEQ, L, CONV], BF16)
    tap_y = tap("y", [NSEQ, L, DI], BF16)
    tap_yg = tap("yg", [NSEQ, L, DI], BF16)
    tap_att = tap("att", [NSEQ, L, DM])

    def phase_a_group(s, g):
        base_tok = g * GRP
        xnb_tr = []
        for c in range(NCH):
            ucol = uc[:, (c % 2) * DM:(c % 2 + 1) * DM]
            cx.dma("scalar", ucol, u[s, base_tok + c * Q: base_tok + (c + 1) * Q, :], "kld")
            uld = cx.mark("kld")
            cx.wait_mark("scalar", uld)
            cx.op("scalar", lambda e, ucol=ucol: e.activation(
                sq[:, :], ucol, Act.Square, accum_out=stat[:, 0:1]), inc="sc")
            sq_m = cx.mark("sc")
            cx.wait_mark("vector", uld)
            cx.op("vector", lambda e, ucol=ucol: e.reduce_sum(stat[:, 8:9], ucol, axis=AX), inc="vc")
            cx.wait_mark("vector", sq_m)
            cx.wait("vector", "vc")
            cx.op("vector", lambda e: e.tensor_scalar_mul(stat[:, 2:3], stat[:, 8:9], 1.0 / DM), inc="vc")
            cx.op("vector", lambda e: e.tensor_scalar_mul(stat[:, 3:4], stat[:, 8:9], -1.0 / DM), inc="vc")
            cx.op("vector", lambda e: e.tensor_scalar_mul(stat[:, 4:5], stat[:, 0:1], 1.0 / DM), inc="vc")
            cx.wait("vector", "vc")
            cx.op("vector", lambda e: e.scalar_tensor_tensor(
                stat[:, 5:6], stat[:, 2:3], stat[:, 2:3], stat[:, 4:5], Alu.mult, Alu.subtract), inc="vc")
            cx.wait("vector", "vc")
            cx.op("vector", lambda e: e.tensor_scalar(stat[:, 6:7], stat[:, 5:6], -1.0, EPS, Alu.mult, Alu.add), inc="vc")
            cx.wait("vector", "vc")
            cx.op("vector", lambda e: e.reciprocal(stat[:, 7:8], stat[:, 6:7]), inc="vc")
            rec_m = cx.mark("vc")
            cx.wait_mark("scalar", rec_m)
            cx.op("scalar", lambda e: e.activation(stat[:, 9:10], stat[:, 7:8], Act.Sqrt), inc="sc")
            sqrt_m = cx.mark("sc")
            cx.wait_mark("vector", sqrt_m)
            if c >= 2:
                cx.wait_mark("vector", xnb_tr[c - 2])
            cx.op("vector", lambda e, ucol=ucol, c=c: e.tensor_scalar(
                xnb[:, (c % 2) * DM:(c % 2 + 1) * DM], ucol, stat[:, 3:4], stat[:, 9:10], Alu.add, Alu.mult), inc="vc")
            stat_free = cx.mark("vc")
            cx.wait_mark("scalar", stat_free)
            cx.wait_mark("sync", stat_free)
            for fc in range(8):
                cx.dmat("sync", xnF[:, fc * GRP + c * Q: fc * GRP + (c + 1) * Q],
                        xnb[:, (c % 2) * DM + fc * 128: (c % 2) * DM + (fc + 1) * 128], "tr")
            if tap_xn is not None:
                cx.dma("sync", tap_xn[s, base_tok + c * Q:base_tok + (c + 1) * Q, :],
                       xnb[:, (c % 2) * DM:(c % 2 + 1) * DM], "st")
            xnb_tr.append(cx.mark("tr"))
        xnF_done = cx.mark("tr")

        cx.wait_mark("tensor", xnF_done)
        evac_marks = []
        for oc in range(33):
            buf = oc % 2
            wslice = winbuf[:, buf * 1024:(buf + 1) * 1024]
            if oc >= 2:
                cx.wait("gpsimd", "tc")
            cx.dma("gpsimd", wslice.rearrange("f (c o) -> f c o", c=8), dr["wint"][oc].rearrange("c f o -> f c o"), "gld")
            cx.wait("tensor", "gld")
            psb = ps_win[oc % 2]
            if oc >= 2:
                cx.wait_mark("tensor", evac_marks[oc - 2])
            for fc in range(8):
                cx.op("tensor", lambda e, fc=fc, psb=psb, wslice=wslice: e.matmul(
                    psb[:, :], wslice[:, fc * 128:(fc + 1) * 128], xnF[:, fc * GRP:(fc + 1) * GRP],
                    start=(fc == 0), stop=(fc == 7)), inc="tc" if fc == 7 else None)
            mmark = cx.mark("tc")
            if oc < 16:
                cx.wait_mark("scalar", mmark)
                cx.op("scalar", lambda e, oc=oc, psb=psb: e.activation(
                    siluzF[:, oc * GRP:(oc + 1) * GRP], psb[:, :], Act.Silu), inc="sc")
                evac_marks.append(cx.mark("sc"))
            elif oc < 32:
                xi = oc - 16
                cx.wait_mark("vector", mmark)
                cx.op("vector", lambda e, xi=xi, psb=psb: e.tensor_copy(
                    xbcF[:, xi * (GRP + 4) + 3: xi * (GRP + 4) + 3 + GRP], psb[:, :]), inc="vc")
                evac_marks.append(cx.mark("vc"))
            else:
                cx.wait_mark("vector", mmark)
                cx.op("vector", lambda e, psb=psb: e.tensor_copy(
                    xbcF[0:16, 16 * (GRP + 4) + 3: 16 * (GRP + 4) + 3 + GRP], psb[0:16, :]), inc="vc")
                cx.op("vector", lambda e, psb=psb: e.tensor_copy(
                    xbcF[32:48, 16 * (GRP + 4) + 3: 16 * (GRP + 4) + 3 + GRP], psb[32:48, :]), inc="vc")
                cx.wait_mark("scalar", mmark)
                cx.op("scalar", lambda e, psb=psb: e.activation(
                    dtF[64:128, :], psb[64:128, :], Act.Exp, bias=c_dtb[64:128, :]), inc="sc")
                cx.wait("scalar", "sc")
                cx.op("scalar", lambda e: e.activation(
                    dtF[64:128, :], dtF[64:128, :], Act.Ln, bias=1.0), inc="sc")
                evac_marks.append(cx.mark("vc"))
        win_done_v = cx.mark("vc")
        win_done_s = cx.mark("sc")
        if tap_xbcpre is not None:
            cx.wait_mark("sync", win_done_v)
            for ci in range(16):
                cx.dma("sync", tap_xbcpre[s, base_tok:base_tok + GRP, ci * 128:(ci + 1) * 128].rearrange("t f -> f t"),
                       xbcF[:, ci * (GRP + 4) + 3: ci * (GRP + 4) + 3 + GRP], "st", allow_slow_non_contiguous=True)
            cx.dma("sync", tap_xbcpre[s, base_tok:base_tok + GRP, 2048:2064].rearrange("t f -> f t"),
                   xbcF[0:16, 16 * (GRP + 4) + 3: 16 * (GRP + 4) + 3 + GRP], "st", allow_slow_non_contiguous=True)
            cx.dma("sync", tap_xbcpre[s, base_tok:base_tok + GRP, 2064:2080].rearrange("t f -> f t"),
                   xbcF[32:48, 16 * (GRP + 4) + 3: 16 * (GRP + 4) + 3 + GRP], "st", allow_slow_non_contiguous=True)
        if tap_siluz is not None:
            cx.wait_mark("sync", win_done_s)
            for ci in range(16):
                cx.dma("sync", tap_siluz[s, base_tok:base_tok + GRP, ci * 128:(ci + 1) * 128].rearrange("t f -> f t"),
                       siluzF[:, ci * GRP:(ci + 1) * GRP], "st", allow_slow_non_contiguous=True)
        cx.wait_mark("vector", win_done_s)
        cx.op("vector", lambda e: e.tensor_scalar_mul(abar[64:128, :], dtF[64:128, :], c_A[64:128, 0:1]), inc="vc")
        cx.wait("vector", "vc")
        for c in range(NCH):
            cx.op("vector", lambda e, c=c: e.tensor_tensor_scan(
                acum[64:128, c * Q:(c + 1) * Q], abar[64:128, c * Q:(c + 1) * Q],
                abar[64:128, c * Q:(c + 1) * Q], 0.0, Alu.add, Alu.bypass), inc="vc")
        dt_done = cx.mark("vc")
        if tap_dt is not None:
            cx.wait_mark("sync", dt_done)
            cx.dma("sync", tap_dt[s, :, base_tok:base_tok + GRP], dtF[64:128, :], "st")
        if tap_acum is not None:
            cx.wait_mark("sync", dt_done)
            for c in range(NCH):
                cx.dma("sync", tap_acum[s, g * NCH + c, :, :], acum[64:128, c * Q:(c + 1) * Q], "st")

        cx.wait_mark("tensor", win_done_v)
        cx.wait_mark("tensor", win_done_s)
        conv_marks = []
        diag_marks = []
        for ci in range(17):
            psb = ps_conv[ci % 2]
            slot = (ci % 2) * 4
            if ci >= 2:
                cx.wait_mark("vector", conv_marks[ci - 2])
            for k in range(4):
                cx.op("vector", lambda e, ci=ci, k=k, slot=slot: e.tensor_scalar_mul(
                    convdiag[:, (slot + k) * 128:(slot + k + 1) * 128], c_identb[:, :],
                    c_convw[:, (ci * 4 + k):(ci * 4 + k) + 1]), inc="vc")
            diag_marks.append(cx.mark("vc"))
            cx.wait_mark("tensor", diag_marks[ci])
            if ci >= 2:
                cx.wait_mark("tensor", conv_marks[ci - 2])
            for k in range(4):
                cx.op("tensor", lambda e, ci=ci, k=k, slot=slot, psb=psb: e.matmul(
                    psb[:, :], convdiag[:, (slot + k) * 128:(slot + k + 1) * 128],
                    xbcF[:, ci * (GRP + 4) + k: ci * (GRP + 4) + k + GRP],
                    start=(k == 0), stop=(k == 3)), inc="tc" if k == 3 else None)
            cmark = cx.mark("tc")
            cx.wait_mark("scalar", cmark)
            if ci < 16:
                cx.op("scalar", lambda e, ci=ci, psb=psb: e.activation(
                    convxF[:, ci * GRP:(ci + 1) * GRP], psb[:, :], Act.Silu,
                    bias=c_convb[:, ci:ci + 1]), inc="sc")
            else:
                cx.op("scalar", lambda e, psb=psb: e.activation(
                    bct[0:16, :], psb[0:16, :], Act.Silu, bias=c_convb[0:16, 16:17]), inc="sc")
                cx.op("scalar", lambda e, psb=psb: e.activation(
                    bct[32:48, :], psb[32:48, :], Act.Silu, bias=c_convb[32:48, 16:17]), inc="sc")
            conv_marks.append(cx.mark("sc"))
        conv_done = cx.mark("sc")
        cx.wait_mark("sync", conv_done)
        cx.dma("sync", ctile0[0:16, :], bct[32:48, :], "tr", s2s=True)
        cx.wait_mark("vector", conv_done)
        cx.wait("vector", "vc")
        for ci in range(17):
            cx.op("vector", lambda e, ci=ci: e.tensor_copy(
                xbcF[:, ci * (GRP + 4): ci * (GRP + 4) + 3],
                xbcF[:, ci * (GRP + 4) + GRP: ci * (GRP + 4) + GRP + 3]), inc="vc")
        if tap_xbc is not None:
            cx.wait_mark("sync", conv_done)
            for ci in range(16):
                cx.dma("sync", tap_xbc[s, base_tok:base_tok + GRP, ci * 128:(ci + 1) * 128].rearrange("t f -> f t"),
                       convxF[:, ci * GRP:(ci + 1) * GRP], "st", allow_slow_non_contiguous=True)
            cx.dma("sync", tap_xbc[s, base_tok:base_tok + GRP, 2048:2064].rearrange("t f -> f t"), bct[0:16, :], "st", allow_slow_non_contiguous=True)
            cx.dma("sync", tap_xbc[s, base_tok:base_tok + GRP, 2064:2080].rearrange("t f -> f t"), bct[32:48, :], "st", allow_slow_non_contiguous=True)
        return conv_done


    def ssd_chunk(s, g, c, dt_done, conv_done, parts=("stage", "g", "r", "states", "ey", "gate", "rms")):
        cq = c * Q
        base_tok = g * GRP + cq
        ach = acum[64:128, cq:cq + Q]
        cx.wait("tensor", "vc")
        cx.wait("tensor", "sc")
        cx.wait_mark("vector", dt_done)
        cx.op("vector", lambda e: e.tensor_scalar(
            decayF[64:128, :], ach, -1.0, acum[64:128, cq + Q - 1:cq + Q], Alu.mult, Alu.add), inc="vc")
        d0_m = cx.mark("vc")
        cx.wait_mark("scalar", d0_m)
        cx.op("scalar", lambda e: e.activation(decayF[64:128, :], decayF[64:128, :], Act.Exp), inc="sc")
        cx.wait_mark("scalar", dt_done)
        cx.op("scalar", lambda e: e.activation(expacumF[64:128, :], ach, Act.Exp), inc="sc")
        cx.op("scalar", lambda e: e.activation(decendcol[64:128, :], acum[64:128, cq + Q - 1:cq + Q], Act.Exp), inc="sc")
        dec_m = cx.mark("sc")
        cx.wait_mark("tensor", dt_done)
        cx.wait_mark("tensor", dec_m)
        cx.op("tensor", lambda e: e.transpose(pb[4][:, 0:64], ach, c_identf[64:128, 64:128]), inc="tc")
        cx.op("tensor", lambda e: e.transpose(pb[4][:, 64:128], dtF[64:128, cq:cq + Q], c_identf[64:128, 64:128]), inc="tc")
        cx.op("tensor", lambda e: e.transpose(pb[4][:, 128:192], decayF[64:128, :], c_identf[64:128, 64:128]), inc="tc")
        tr_m = cx.mark("tc")
        cx.wait_mark("vector", tr_m)
        cx.op("vector", lambda e: e.tensor_scalar_mul(negAcumT[:, :], pb[4][:, 0:64], -1.0), inc="vc")
        cx.op("vector", lambda e: e.tensor_copy(dtT[:, :], pb[4][:, 64:128]), inc="vc")
        cx.op("vector", lambda e: e.tensor_copy(decayT[:, :], pb[4][:, 128:192]), inc="vc")
        trc_m = cx.mark("vc")
        cx.wait_mark("sync", conv_done)
        for ci in range(16):
            cx.dmat("sync", x_T[:, ci * 128:(ci + 1) * 128],
                    convxF[:, ci * GRP + cq: ci * GRP + cq + Q], "tr")
        cx.dmat("sync", bt_tok[:, 0:16], bct[0:16, cq:cq + Q], "tr")
        cx.wait_mark("sync", dt_done)
        cx.wait_mark("vector", dt_done)
        cx.wait("vector", "tr")
        cx.op("vector", lambda e: e.tensor_copy(ahib[64:128, :], ach), inc="vc")
        cx.wait("vector", "vc")
        cx.op("vector", lambda e: e.scalar_tensor_tensor(
            tA[64:128, :], ahib[64:128, :], -1.0, ach, Alu.mult, Alu.add), inc="vc")
        cx.wait("vector", "vc")
        cx.op("vector", lambda e: e.tensor_copy(alob[64:128, :], tA[64:128, :]), inc="vc")
        cx.wait("vector", "vc")
        cx.op("vector", lambda e: e.scalar_tensor_tensor(
            alo2b[64:128, :], alob[64:128, :], -1.0, tA[64:128, :], Alu.mult, Alu.add), inc="vc")
        split_m = cx.mark("vc")
        cx.wait_mark("sync", split_m)
        for r in range(3):
            cnt = (22, 22, 20)[r]
            for fl, tile in ((aflatH, ahib), (aflatL, alob), (aflatL2, alo2b)):
                cx.dma("sync", fl[32 * r:32 * r + 1, 0:cnt * Q],
                       tile[64 + 22 * r:64 + 22 * r + cnt, :], "tr", s2s=True)
        cx.wait_mark("sync", dec_m)
        for r in range(3):
            cnt = (22, 22, 20)[r]
            cx.dma("sync", expR3[r:r + 1, 0:cnt * Q],
                   expacumF[64 + 22 * r:64 + 22 * r + cnt, :], "tr", s2s=True)
        cx.wait_mark("sync", conv_done)
        for r in range(3):
            cx.dma("sync", ccrep[32 * r:32 * r + 16, :], ctile0[0:16, cq:cq + Q], "tr", s2s=True)
        for r in range(3):
            cnt = (22, 22, 20)[r]
            cx.dma("sync", prevR[32 * r:32 * r + 16, 0:cnt * 32],
                   prevb[0:16, 22 * r * 32:(22 * r + cnt) * 32], "tr", s2s=True)
        cx.dma("sync", decrow[0:1, :], decendcol[64:128, 0:1], "tr", s2s=True)
        dcr_m = cx.mark("tr")
        cx.wait_mark("tensor", dcr_m)
        cx.wait_mark("tensor", trc_m)
        cx.op("tensor", lambda e: e.matmul(pb[4][0:16, 320:384], c_onesf[0:1, 0:16],
                                           decrow[0:1, :], start=True, stop=True), inc="tc")
        dce_m = cx.mark("tc")
        cx.wait_mark("vector", dce_m)
        cx.op("vector", lambda e: e.tensor_copy(decend[0:16, :], pb[4][0:16, 320:384]), inc="vc")
        stage_m = cx.mark("tr")
        cx.wait_mark("vector", stage_m)
        cx.wait_mark("vector", trc_m)
        cx.op("vector", lambda e: e.tensor_tensor(
            xdt_T.ap().rearrange("p (h w) -> p h w", w=32), x_T.ap().rearrange("p (h w) -> p h w", w=32),
            dtT.ap().unsqueeze(2).broadcast_to([128, 64, 32]), Alu.mult), inc="vc")
        cx.wait("vector", "vc")
        cx.op("vector", lambda e: e.tensor_tensor(
            xdd_T.ap().rearrange("p (h w) -> p h w", w=32), xdt_T.ap().rearrange("p (h w) -> p h w", w=32),
            decayT.ap().unsqueeze(2).broadcast_to([128, 64, 32]), Alu.mult), inc="vc")
        xdd_m = cx.mark("vc")
        if "g" not in parts:
            return cx.mark("vc")
        cx.wait_mark("tensor", conv_done)
        cx.wait("tensor", "tr")
        cx.op("tensor", lambda e: e.matmul(pb[5][:, 0:128], bct[0:16, cq:cq + Q],
                                           ctile0[0:16, cq:cq + Q], start=True, stop=True), inc="tc")
        g_m = cx.mark("tc")
        cx.wait_mark("vector", g_m)
        cx.op("vector", lambda e: e.tensor_copy(stile[:, :], pb[5][:, 0:128]), inc="vc")
        st_m = cx.mark("vc")
        if "r" not in parts:
            return cx.mark("vc")
        cx.wait_mark("tensor", st_m)
        r_marks = {}
        for rg in range(22):
            bk = pb[5 + (rg % 2)]
            if rg >= 2:
                cx.wait_mark("tensor", r_marks[rg - 2])
            cx.op("tensor", lambda e, rg=rg, bk=bk: e.matmul(
                bk[:, 0:128], c_delta3[0:4, :],
                expR3[0:4, rg * Q:(rg + 1) * Q], start=True, stop=True), inc="tc")
            rmm = cx.mark("tc")
            cx.wait_mark("vector", rmm)
            cx.op("vector", lambda e, rg=rg, bk=bk: e.tensor_tensor(
                r3[:, rg * Q:(rg + 1) * Q], bk[:, 0:128],
                ccrep[:, :], Alu.mult), inc="vc")
            r_marks[rg] = cx.mark("vc")
        r_done = cx.mark("vc")
        if "states" not in parts:
            return cx.mark("vc")
        cx.wait_mark("tensor", xdd_m)
        st_mm = []
        for j in range(4):
            cx.op("tensor", lambda e, j=j: e.matmul(
                pb[j][0:16, :], bt_tok[:, 0:16], xdd_T[:, j * GRP:(j + 1) * GRP],
                start=True, stop=True), inc="tc")
            st_mm.append(cx.mark("tc"))
        cx.wait_mark("vector", stage_m)
        for j in range(4):
            cx.op("vector", lambda e, j=j: e.tensor_tensor(
                prev0.ap().rearrange("p (h w) -> p h w", w=32)[0:16, j * 16:(j + 1) * 16, :],
                prev0.ap().rearrange("p (h w) -> p h w", w=32)[0:16, j * 16:(j + 1) * 16, :],
                decend.ap().unsqueeze(2).broadcast_to([16, 64, 32])[0:16, j * 16:(j + 1) * 16, :], Alu.mult), inc="vc")
        cx.wait("vector", "vc")
        for j in range(4):
            cx.wait_mark("vector", st_mm[j])
            cx.op("vector", lambda e, j=j: e.tensor_tensor(
                prev0[0:16, j * GRP:(j + 1) * GRP], prev0[0:16, j * GRP:(j + 1) * GRP],
                pb[j][0:16, :], Alu.add), inc="vc")
        cx.wait("vector", "vc")
        cx.op("vector", lambda e: e.tensor_copy(prevb[0:16, :], prev0[0:16, :]), inc="vc")
        rec_done = cx.mark("vc")
        if "ey" not in parts:
            return cx.mark("vc")
        yhalf_marks = []
        e_vmarks = {}
        e_smarks = {}
        y_mm_marks = {}
        y_bank_free = [rec_done] * 4
        cx.wait_mark("tensor", r_done)
        cx.wait_mark("tensor", rec_done)
        for half in range(2):
            for w in range(2):
                wave = half * 2 + w
                for b in range(4):
                    grp = wave * 4 + b
                    ep = pb[6 + (grp % 2)]
                    if grp >= 2 and (grp - 2) in e_vmarks:
                        cx.wait_mark("tensor", e_vmarks[grp - 2])
                        cx.wait_mark("tensor", e_smarks[grp - 2])
                    h0 = grp * 4
                    segs = []
                    j = 0
                    while j < 4:
                        h = h0 + j
                        b3 = 32 * (h // 22)
                        k = 1
                        while j + k < 4 and (h0 + j + k) // 22 == h // 22:
                            k += 1
                        segs.append((j, k, b3, h % 22))
                        j += k
                    for (j, k, b3, blk) in segs:
                        cx.op("tensor", lambda e, ep=ep, j=j, k=k: e.matmul(
                            ep[:, j * 128:(j + k) * 128], c_identb[:, :],
                            c_mask4[:, j * 128:(j + k) * 128], start=True, stop=False), inc=None)
                        for fi, fl in enumerate((aflatH, aflatL, aflatL2)):
                            last = (fi == 2) and (j + k == 4)
                            cx.op("tensor", lambda e, ep=ep, j=j, k=k, b3=b3, blk=blk, fl=fl, fi=fi, last=last: e.matmul(
                                ep[:, j * 128:(j + k) * 128], c_onesallb[b3:b3 + 1, 0:128],
                                fl[b3:b3 + 1, blk * Q:(blk + k) * Q],
                                start=False, stop=(fi == 2)), inc="tc" if last else None)
                    emm = cx.mark("tc")
                    cx.wait_mark("scalar", emm)
                    if (grp - 1) in e_vmarks:
                        cx.wait_mark("scalar", e_vmarks[grp - 1])
                    for j in range(4):
                        h = grp * 4 + j
                        cx.op("scalar", lambda e, ep=ep, j=j, h=h: e.activation(
                            etile[:, j * 128:(j + 1) * 128], ep[:, j * 128:(j + 1) * 128],
                            Act.Exp, bias=negAcumT[:, h:h + 1]), inc="sc")
                    e_smarks[grp] = cx.mark("sc")
                    cx.wait_mark("vector", e_smarks[grp])
                    cx.wait_mark("vector", st_m)
                    if (grp - 1) in y_mm_marks:
                        cx.wait_mark("vector", y_mm_marks[grp - 1])
                    cx.op("vector", lambda e: e.tensor_tensor(
                        emtile.ap().rearrange("p (j l) -> p j l", l=Q), etile.ap().rearrange("p (j l) -> p j l", l=Q),
                        stile.ap().unsqueeze(1).broadcast_to([128, 4, 128]), Alu.mult), inc="vc")
                    e_vmarks[grp] = cx.mark("vc")
                    cx.wait_mark("tensor", e_vmarks[grp])
                    cx.wait_mark("tensor", r_done)
                    cx.wait_mark("tensor", y_bank_free[b])
                    for j in range(4):
                        h = grp * 4 + j
                        b3 = 32 * (h // 22)
                        blk = h % 22
                        cx.op("tensor", lambda e, b=b, j=j, h=h: e.matmul(
                            pb[b][0:32, j * 128:(j + 1) * 128], xdt_T[:, h * 32:(h + 1) * 32],
                            emtile[:, j * 128:(j + 1) * 128], start=(j == 0), stop=False), inc=None)
                        last = (j == 3)
                        cx.op("tensor", lambda e, b=b, j=j, b3=b3, blk=blk, last=last: e.matmul(
                            pb[b][0:32, j * 128:(j + 1) * 128],
                            prevR[b3:b3 + 16, blk * 32:blk * 32 + 32],
                            r3[b3:b3 + 16, blk * Q:(blk + 1) * Q],
                            start=False, stop=last), inc="tc" if last else None)
                    ymm = cx.mark("tc")
                    y_mm_marks[grp] = ymm
                    cx.wait_mark("vector", ymm)
                    if yhalf_marks:
                        cx.wait_mark("vector", yhalf_marks[-1])
                    hh0 = b * 4
                    cx.op("vector", lambda e, b=b, hh0=hh0: e.tensor_copy(
                        yP[0:32, hh0 * 128:(hh0 + 4) * 128], pb[b][0:32, 0:512]), inc="vc")
                    y_bank_free[b] = cx.mark("vc")
                yh = cx.mark("vc")
                cx.wait_mark("sync", yh)
                for h4 in range(4):
                    cx.dma("sync", yF[32 * h4:32 * h4 + 32, wave * 4 * Q:(wave * 4 + 4) * Q],
                           yP[0:32, :].rearrange("p (h16 h4p l) -> p h16 h4p l", h4p=4, l=Q)[:, :, h4, :], "tr", s2s=True)
                yhalf_marks.append(cx.mark("tr"))
        y_done = cx.mark("tr")
        if "gate" not in parts:
            return cx.mark("vc")
        cx.wait_mark("vector", y_done)
        for ci in range(16):
            cx.op("vector", lambda e, ci=ci: e.scalar_tensor_tensor(
                yg[:, ci * 128:(ci + 1) * 128], convxF[:, ci * GRP + cq: ci * GRP + cq + Q],
                c_dvec[:, ci:ci + 1], yF[:, ci * 128:(ci + 1) * 128], Alu.mult, Alu.add), inc="vc")
        cx.wait("vector", "vc")
        for ci in range(16):
            cx.op("vector", lambda e, ci=ci: e.tensor_tensor(
                yg[:, ci * 128:(ci + 1) * 128], yg[:, ci * 128:(ci + 1) * 128],
                siluzF[:, ci * GRP + cq: ci * GRP + cq + Q], Alu.mult), inc="vc")
        yg_m = cx.mark("vc")
        cx.wait("vector", "vc")
        for ci in range(16):
            cx.op("vector", lambda e, ci=ci, c=c: e.tensor_scalar_mul(
                ygw[:, ci * GRP + cq: ci * GRP + cq + Q], yg[:, ci * 128:(ci + 1) * 128],
                c_normw[:, ci:ci + 1]), inc="vc")
        ygw_m = cx.mark("vc")
        if "rms" not in parts:
            return cx.mark("vc")
        cx.wait_mark("scalar", yg_m)
        for ci in range(16):
            cx.op("scalar", lambda e, ci=ci: e.activation(
                ygsq[:, ci * 128:(ci + 1) * 128], yg[:, ci * 128:(ci + 1) * 128], Act.Square), inc="sc")
        ygsq_m = cx.mark("sc")
        cx.wait_mark("tensor", ygsq_m)
        for ci in range(16):
            cx.op("tensor", lambda e, ci=ci: e.matmul(
                pb[4][0:1, 256:384], c_onescolb[:, 0:1], ygsq[:, ci * 128:(ci + 1) * 128],
                start=(ci == 0), stop=(ci == 15)), inc="tc" if ci == 15 else None)
        rms_m = cx.mark("tc")
        cx.wait_mark("vector", rms_m)
        cx.op("vector", lambda e: e.tensor_scalar(
            rmsrow[0:1, :], pb[4][0:1, 256:384], 1.0 / DI, EPS, Alu.mult, Alu.add), inc="vc")
        cx.wait("vector", "vc")
        cx.op("vector", lambda e: e.reciprocal(rmsrow[0:1, :], rmsrow[0:1, :]), inc="vc")
        rr_m = cx.mark("vc")
        cx.wait_mark("scalar", rr_m)
        cx.op("scalar", lambda e: e.activation(rmsrow[0:1, :], rmsrow[0:1, :], Act.Sqrt), inc="sc")
        rs_m = cx.mark("sc")
        cx.wait_mark("sync", rs_m)
        cx.dma("sync", invrg[:, c:c + 1], rmsrow[0:1, :], "tr", s2s=True)
        if tap_y is not None:
            cx.wait_mark("sync", y_done)
            for ci in range(16):
                cx.dma("sync", tap_y[s, base_tok:base_tok + Q, ci * 128:(ci + 1) * 128].rearrange("t f -> f t"),
                       yF[:, ci * 128:(ci + 1) * 128], "st", allow_slow_non_contiguous=True)
        if tap_yg is not None:
            cx.wait_mark("sync", yg_m)
            for ci in range(16):
                cx.dma("sync", tap_yg[s, base_tok:base_tok + Q, ci * 128:(ci + 1) * 128].rearrange("t f -> f t"),
                       yg[:, ci * 128:(ci + 1) * 128], "st", allow_slow_non_contiguous=True)
        return ygw_m

    def wout_group(s, g, ygw_m):
        base_tok = g * GRP
        cx.wait_mark("tensor", ygw_m)
        ev_marks = []
        for dh in range(2):
            for f in range(16):
                buf = (dh * 16 + f) % 2
                wsl = wobuf[:, buf * 512:(buf + 1) * 512]
                if dh * 16 + f >= 2:
                    cx.wait("gpsimd", "tc")
                cx.dma("gpsimd", wsl, dr["woutt"][f * 128:(f + 1) * 128, dh * 512:(dh + 1) * 512], "gld")
                cx.wait("tensor", "gld")
                if f == 0 and len(ev_marks) >= 1:
                    for m in ev_marks[-NCH:]:
                        cx.wait_mark("tensor", m)
                for c in range(NCH):
                    cx.op("tensor", lambda e, c=c, f=f, wsl=wsl: e.matmul(
                        pb[c][:, :], ygw[:, f * GRP + c * Q: f * GRP + (c + 1) * Q], wsl,
                        start=(f == 0), stop=(f == 15)), inc="tc" if c == NCH - 1 else None)
            mm_m = cx.mark("tc")
            for c in range(NCH):
                cx.wait("scalar", "vc")
                cx.dma("scalar", ures[:, dh * 512:(dh + 1) * 512],
                       u[s, base_tok + c * Q:base_tok + (c + 1) * Q, dh * 512:(dh + 1) * 512], "kld")
                uld = cx.mark("kld")
                cx.wait_mark("vector", uld)
                cx.wait_mark("vector", mm_m)
                cx.wait("vector", "st")
                cx.op("vector", lambda e, c=c, dh=dh: e.scalar_tensor_tensor(
                    att[:, dh * 512:(dh + 1) * 512], pb[c][:, :], invrg[:, c:c + 1],
                    ures[:, dh * 512:(dh + 1) * 512], Alu.mult, Alu.add), inc="vc")
                evm = cx.mark("vc")
                ev_marks.append(evm)
                cx.wait_mark("sync", evm)
                cx.dma("sync", outacc[s, base_tok + c * Q:base_tok + (c + 1) * Q, dh * 512:(dh + 1) * 512],
                       att[:, dh * 512:(dh + 1) * 512], "st")
                if tap_att is not None:
                    cx.dma("sync", tap_att[s, base_tok + c * Q:base_tok + (c + 1) * Q, dh * 512:(dh + 1) * 512],
                           att[:, dh * 512:(dh + 1) * 512], "st")
        return cx.mark("st")

    for s in range(NSEQ):
        cx.op("vector", lambda e: e.memset(prev0[:, :], 0.0), inc="vc")
        cx.op("vector", lambda e: e.memset(prevb[:, :], 0.0), inc="vc")
        for ci in range(17):
            cx.op("vector", lambda e, ci=ci: e.memset(xbcF[:, ci * (GRP + 4): ci * (GRP + 4) + 3], 0.0), inc="vc")
        for g in range(NGRP):
            conv_done = phase_a_group(s, g)
            dt_done = cx.mark("vc")
            if "ssd" in phases:
                ygw_m = None
                for c in range(NCH):
                    ygw_m = ssd_chunk(s, g, c, dt_done, conv_done, parts=ssd_parts)
                if "wout" in phases:
                    wout_group(s, g, ygw_m)
            cx.barrier()


    tap_xq1 = tap("xq1", [NSEQ, L, DM], BF16)
    tap_gate = tap("gate", [NSEQ, L, 4096], BF16)
    tap_hq = tap("hq", [NSEQ, L, 4096], BF16)

    def quant_weights(wdram, nt, C, mws_col):
        cx.op("vector", lambda e: e.memset(mwc[:, 0:1], 0.0), inc="vc")
        cx.wait("vector", "vc")
        for i in range(nt):
            cx.wait("scalar", "sc")
            cx.wait("scalar", "vc")
            cx.dma("scalar", wtmp[:, 0:C], wdram[i], "kld")
            wl = cx.mark("kld")
            cx.wait_mark("scalar", wl)
            cx.op("scalar", lambda e: e.activation(hpT[:, 0:C], wtmp[:, 0:C], Act.Abs,
                                                   accum_out=mwc[:, 1:2]), inc="sc")
            am = cx.mark("sc")
            cx.wait_mark("vector", am)
            cx.op("vector", lambda e: e.tensor_tensor(mwc[:, 0:1], mwc[:, 0:1], mwc[:, 1:2], Alu.add), inc="vc")
            cx.wait("vector", "vc")
        s_m = cx.mark("vc")
        cx.wait_mark("tensor", s_m)
        cx.op("tensor", lambda e: e.matmul(pb[0][0:1, 0:1], mwc[:, 0:1], c_onesf[:, 0:1],
                                           start=True, stop=True), inc="tc")
        t_m = cx.mark("tc")
        cx.wait_mark("vector", t_m)
        cx.op("vector", lambda e, nt=nt, C=C: e.tensor_scalar(
            mwrow[0:1, 0:1], pb[0][0:1, 0:1], 1.0 / (nt * 128.0 * C), None, Alu.mult), inc="vc")
        cx.wait("vector", "vc")
        cx.op("vector", lambda e: e.tensor_scalar_max(mwrow[0:1, 0:1], mwrow[0:1, 0:1], 1e-5), inc="vc")
        m_m = cx.mark("vc")
        cx.wait_mark("tensor", m_m)
        cx.op("tensor", lambda e: e.matmul(pb[0][:, 1:2], c_onesf[0:1, 0:128], mwrow[0:1, 0:1],
                                           start=True, stop=True), inc="tc")
        b_m = cx.mark("tc")
        cx.wait_mark("vector", b_m)
        cx.op("vector", lambda e: e.tensor_scalar_mul(mwc[:, 4:5], pb[0][:, 1:2], 0.5), inc="vc")
        cx.op("vector", lambda e: e.tensor_scalar_mul(mwc[:, 5:6], pb[0][:, 1:2], -0.5), inc="vc")
        cx.op("vector", lambda e, mws_col=mws_col: e.tensor_scalar_mul(
            mws[:, mws_col:mws_col + 1], pb[0][:, 1:2], -1.0), inc="vc")
        cx.wait("vector", "vc")
        for i in range(nt):
            cx.wait("scalar", "vc")
            cx.dma("scalar", wtmp[:, 0:C], wdram[i], "kld")
            wl = cx.mark("kld")
            cx.wait_mark("vector", wl)
            cx.op("vector", lambda e, C=C: e.tensor_single_scalar(
                qv[:, 0:C] if C <= 2048 else hpT[:, 0:C], wtmp[:, 0:C], mwc[:, 4:5], Alu.is_gt), inc="vc")
            cx.wait("vector", "vc")
            cx.op("vector", lambda e, i=i, C=C: e.scalar_tensor_tensor(
                wslot[:, i * C:(i + 1) * C], wtmp[:, 0:C], mwc[:, 5:6],
                qv[:, 0:C] if C <= 2048 else hpT[:, 0:C], Alu.is_lt, Alu.subtract), inc="vc")
            cx.wait("vector", "vc")
        cx.barrier()

    def ln_quant1(s, ti):
        ucol = uc[:, 0:DM]
        cx.wait("scalar", "vc")
        cx.wait("scalar", "tr")
        cx.dma("scalar", ucol, u[s, ti * 128:(ti + 1) * 128, :], "kld")
        uld = cx.mark("kld")
        cx.wait_mark("scalar", uld)
        cx.op("scalar", lambda e: e.activation(sq[:, :], ucol, Act.Square, accum_out=stat[:, 0:1]), inc="sc")
        sq_m = cx.mark("sc")
        cx.wait_mark("vector", uld)
        cx.op("vector", lambda e: e.reduce_sum(stat[:, 8:9], ucol, axis=AX), inc="vc")
        cx.wait_mark("vector", sq_m)
        cx.wait("vector", "vc")
        cx.op("vector", lambda e: e.tensor_scalar_mul(stat[:, 2:3], stat[:, 8:9], 1.0 / DM), inc="vc")
        cx.op("vector", lambda e: e.tensor_scalar_mul(stat[:, 3:4], stat[:, 8:9], -1.0 / DM), inc="vc")
        cx.op("vector", lambda e: e.tensor_scalar_mul(stat[:, 4:5], stat[:, 0:1], 1.0 / DM), inc="vc")
        cx.wait("vector", "vc")
        cx.op("vector", lambda e: e.scalar_tensor_tensor(
            stat[:, 5:6], stat[:, 2:3], stat[:, 2:3], stat[:, 4:5], Alu.mult, Alu.subtract), inc="vc")
        cx.wait("vector", "vc")
        cx.op("vector", lambda e: e.tensor_scalar(stat[:, 6:7], stat[:, 5:6], -1.0, EPS, Alu.mult, Alu.add), inc="vc")
        cx.wait("vector", "vc")
        cx.op("vector", lambda e: e.reciprocal(stat[:, 7:8], stat[:, 6:7]), inc="vc")
        rec_m = cx.mark("vc")
        cx.wait_mark("scalar", rec_m)
        cx.op("scalar", lambda e: e.activation(stat[:, 9:10], stat[:, 7:8], Act.Sqrt), inc="sc")
        sqrt_m = cx.mark("sc")
        cx.wait_mark("vector", sqrt_m)
        cx.op("vector", lambda e: e.tensor_scalar(
            xnt[:, :], ucol, stat[:, 3:4], stat[:, 9:10], Alu.add, Alu.mult), inc="vc")
        cx.wait("vector", "vc")
        cx.op("vector", lambda e: e.reduce_max(stat[:, 10:11], xnt[:, :], axis=AX,
                                               apply_absolute_value=True), inc="vc")
        cx.wait("vector", "vc")
        cx.op("vector", lambda e: e.tensor_scalar_max(stat[:, 11:12], stat[:, 10:11], 1e-5), inc="vc")
        cx.wait("vector", "vc")
        cx.op("vector", lambda e: e.reciprocal(stat[:, 12:13], stat[:, 11:12]), inc="vc")
        cx.wait("vector", "vc")
        cx.op("vector", lambda e: e.tensor_scalar_mul(stat[:, 13:14], stat[:, 12:13], 127.0), inc="vc")
        cx.op("vector", lambda e: e.tensor_scalar_mul(stat[:, 14:15], stat[:, 11:12], 1.0 / 127.0), inc="vc")
        cx.wait("vector", "vc")
        cx.op("vector", lambda e: e.tensor_scalar_mul(qv[:, 0:DM], xnt[:, :], stat[:, 13:14]), inc="vc")
        cx.wait("vector", "vc")
        cx.op("vector", lambda e: e.tensor_scalar(qa[:, 0:DM], qv[:, 0:DM], M1, M1, Alu.add, Alu.subtract), inc="vc")
        cx.op("vector", lambda e: e.tensor_scalar(qb[:, 0:DM], qv[:, 0:DM], M2, M2, Alu.add, Alu.subtract), inc="vc")
        cx.op("vector", lambda e: e.tensor_scalar(qc[:, 0:DM], qv[:, 0:DM], M3, M3, Alu.add, Alu.subtract), inc="vc")
        cx.wait("vector", "vc")
        cx.op("vector", lambda e: e.tensor_tensor(qa[:, 0:DM], qa[:, 0:DM], qc[:, 0:DM], Alu.subtract), inc="vc")
        cx.wait("vector", "vc")
        cx.op("vector", lambda e: e.scalar_tensor_tensor(
            qa[:, 0:DM], qa[:, 0:DM], vblend[:, 0:1], qb[:, 0:DM], Alu.mult, Alu.add), inc="vc")
        cx.wait("vector", "vc")
        cx.op("vector", lambda e: e.tensor_scalar_mul(xq1t[:, :], qa[:, 0:DM], stat[:, 14:15]), inc="vc")
        q_m = cx.mark("vc")
        cx.wait_mark("sync", q_m)
        cx.wait("sync", "tc")
        for fc in range(8):
            cx.dmat("sync", xq1F[:, fc * 128:(fc + 1) * 128], xq1t[:, fc * 128:(fc + 1) * 128], "tr")
        if tap_xq1 is not None:
            cx.dma("sync", tap_xq1[s, ti * 128:(ti + 1) * 128, :], xq1t[:, :], "st")
        return cx.mark("tr")

    def phase_b1():
        quant_weights(dr["wgt"], 8, 4096, 0)
        ev_m = cx.mark("sc")
        for gt in range(32):
            s, ti = gt // 16, gt % 16
            xf_m = ln_quant1(s, ti)
            cx.dma("sync", xq1scr[gt], xq1F[:, :], "tr")
            cx.wait_mark("tensor", xf_m)
            cx.wait_mark("tensor", ev_m)
            for fc in range(8):
                for ob in range(8):
                    cx.op("tensor", lambda e, fc=fc, ob=ob: e.matmul(
                        pb[ob][:, :], xq1F[:, fc * 128:(fc + 1) * 128],
                        wslot[:, fc * 4096 + ob * 512: fc * 4096 + (ob + 1) * 512],
                        start=(fc == 0), stop=(fc == 7)), inc="tc" if ob == 7 else None)
            mm_m = cx.mark("tc")
            cx.wait_mark("scalar", mm_m)
            cx.wait("scalar", "tr")
            for ob in range(8):
                cx.op("scalar", lambda e, ob=ob: e.activation(
                    gateT[:, ob * 512:(ob + 1) * 512], pb[ob][:, :], Act.Silu,
                    scale=mws[:, 0:1]), inc="sc")
            ev_m = cx.mark("sc")
            cx.wait_mark("sync", ev_m)
            cx.dma("sync", gscr[gt], gateT[:, :], "tr")
            if tap_gate is not None:
                cx.dma("sync", tap_gate[s, ti * 128:(ti + 1) * 128, :], gateT[:, :], "st")
        cx.barrier()

    def phase_b2():
        quant_weights(dr["wut"], 8, 4096, 1)
        ev_m = cx.mark("vc")
        for gt in range(32):
            s, ti = gt // 16, gt % 16
            cx.wait("scalar", "tc")
            cx.dma("scalar", xq1F[:, :], xq1scr[gt], "kld")
            cx.wait("scalar", "vc")
            cx.dma("scalar", gateT[:, :], gscr[gt], "kld")
            ld_m = cx.mark("kld")
            cx.wait_mark("tensor", ld_m)
            cx.wait_mark("tensor", ev_m)
            for fc in range(8):
                for ob in range(8):
                    cx.op("tensor", lambda e, fc=fc, ob=ob: e.matmul(
                        pb[ob][:, :], xq1F[:, fc * 128:(fc + 1) * 128],
                        wslot[:, fc * 4096 + ob * 512: fc * 4096 + (ob + 1) * 512],
                        start=(fc == 0), stop=(fc == 7)), inc="tc" if ob == 7 else None)
            mm_m = cx.mark("tc")
            cx.wait_mark("vector", mm_m)
            cx.wait_mark("vector", ld_m)
            for ob in range(8):
                cx.op("vector", lambda e, ob=ob: e.scalar_tensor_tensor(
                    hpT[:, ob * 512:(ob + 1) * 512], pb[ob][:, :], mws[:, 1:2],
                    gateT[:, ob * 512:(ob + 1) * 512], Alu.mult, Alu.mult), inc="vc")
            ev_m = cx.mark("vc")
            cx.wait("vector", "vc")
            cx.op("vector", lambda e: e.reduce_max(stat[:, 10:11], hpT[:, :], axis=AX,
                                                   apply_absolute_value=True), inc="vc")
            cx.wait("vector", "vc")
            cx.op("vector", lambda e: e.tensor_scalar_max(stat[:, 11:12], stat[:, 10:11], 1e-5), inc="vc")
            cx.wait("vector", "vc")
            cx.op("vector", lambda e: e.reciprocal(stat[:, 12:13], stat[:, 11:12]), inc="vc")
            cx.wait("vector", "vc")
            cx.op("vector", lambda e: e.tensor_scalar_mul(stat[:, 13:14], stat[:, 12:13], 127.0), inc="vc")
            cx.op("vector", lambda e: e.tensor_scalar_mul(stat[:, 14:15], stat[:, 11:12], 1.0 / 127.0), inc="vc")
            cx.wait("vector", "vc")
            for hh in range(2):
                hsl = slice(hh * 2048, (hh + 1) * 2048)
                cx.op("vector", lambda e, hsl=hsl: e.tensor_scalar_mul(qv[:, :], hpT[:, hsl], stat[:, 13:14]), inc="vc")
                cx.wait("vector", "vc")
                cx.op("vector", lambda e: e.tensor_scalar(qa[:, :], qv[:, :], M1, M1, Alu.add, Alu.subtract), inc="vc")
                cx.op("vector", lambda e: e.tensor_scalar(qb[:, :], qv[:, :], M2, M2, Alu.add, Alu.subtract), inc="vc")
                cx.op("vector", lambda e: e.tensor_scalar(qc[:, :], qv[:, :], M3, M3, Alu.add, Alu.subtract), inc="vc")
                cx.wait("vector", "vc")
                cx.op("vector", lambda e: e.tensor_tensor(qa[:, :], qa[:, :], qc[:, :], Alu.subtract), inc="vc")
                cx.wait("vector", "vc")
                cx.op("vector", lambda e: e.scalar_tensor_tensor(
                    qa[:, :], qa[:, :], vblend[:, 0:1], qb[:, :], Alu.mult, Alu.add), inc="vc")
                cx.wait("vector", "vc")
                cx.op("vector", lambda e, hsl=hsl: e.tensor_scalar_mul(xq2t[:, hsl], qa[:, :], stat[:, 14:15]), inc="vc")
                cx.wait("vector", "vc")
            q_m = cx.mark("vc")
            cx.wait_mark("sync", q_m)
            for oc in range(32):
                cx.dmat("sync", stripb[:, oc * 128:(oc + 1) * 128], xq2t[:, oc * 128:(oc + 1) * 128], "tr")
            sp_m = cx.mark("tr")
            cx.dma("sync", xq2scr[gt], stripb[:, :], "tr")
            if tap_hq is not None:
                cx.dma("sync", tap_hq[s, ti * 128:(ti + 1) * 128, :], xq2t[:, :], "st")
        cx.barrier()

    def phase_c():
        quant_weights(dr["wdt"], 32, 1024, 2)
        ev_m = cx.mark("vc")
        for gt in range(32):
            s, ti = gt // 16, gt % 16
            cx.wait("scalar", "tc")
            cx.dma("scalar", stripb[:, :], xq2scr[gt], "kld")
            cx.wait("scalar", "vc")
            cx.dma("scalar", ures[:, :], outacc[s, ti * 128:(ti + 1) * 128, :], "kld")
            ld_m = cx.mark("kld")
            cx.wait_mark("tensor", ld_m)
            cx.wait_mark("tensor", ev_m)
            for oc in range(32):
                for dh in range(2):
                    cx.op("tensor", lambda e, oc=oc, dh=dh: e.matmul(
                        pb[dh][:, :], stripb[:, oc * 128:(oc + 1) * 128],
                        wslot[:, oc * 1024 + dh * 512: oc * 1024 + (dh + 1) * 512],
                        start=(oc == 0), stop=(oc == 31)), inc="tc" if dh == 1 else None)
            mm_m = cx.mark("tc")
            cx.wait_mark("vector", mm_m)
            cx.wait_mark("vector", ld_m)
            cx.wait("vector", "st")
            for dh in range(2):
                cx.op("vector", lambda e, dh=dh: e.scalar_tensor_tensor(
                    att[:, dh * 512:(dh + 1) * 512], pb[dh][:, :], mws[:, 2:3],
                    ures[:, dh * 512:(dh + 1) * 512], Alu.mult, Alu.add), inc="vc")
            ev_m = cx.mark("vc")
            cx.wait_mark("sync", ev_m)
            cx.dma("sync", out[s, ti * 128:(ti + 1) * 128, :], att[:, :], "st")
        cx.barrier()

    if "B" in phases:
        phase_b1()
        phase_b2()
        phase_c()

    cx.wait("sync", "st")
    cx.wait("sync", "tr")
    cx.finalize()
    stack.close()
    return nc, taps


def host_inputs(inp, core):
    u_full = np.ascontiguousarray(inp["L0_attn_u"])
    win = np.asarray(inp["L0_attn_W_in"])
    wint = np.zeros((4224, 1024), np.float32)
    wint[:4096] = win[:4096]
    wint[4096:4112] = win[4096:4112]
    wint[4128:4144] = win[4112:4128]
    wint[4160:4224] = win[4128:4192]
    wint = np.ascontiguousarray(wint.T)
    wint_t = np.zeros((33, 8, 128, 128), np.float32)
    for oc in range(33):
        for fc in range(8):
            wint_t[oc, fc] = wint[fc * 128:(fc + 1) * 128, oc * 128:(oc + 1) * 128]
    woutt = np.ascontiguousarray(np.asarray(inp["L0_attn_W_out"]).T)
    cw = np.asarray(inp["L0_attn_conv_w"]).reshape(2080, 4)
    convw = np.zeros((128, 68), np.float32)
    convb = np.zeros((128, 17), np.float32)
    cb = np.asarray(inp["L0_attn_conv_b"])
    for ci in range(16):
        for k in range(4):
            convw[:, ci * 4 + k] = cw[ci * 128:(ci + 1) * 128, k]
        convb[:, ci] = cb[ci * 128:(ci + 1) * 128]
    for k in range(4):
        convw[0:16, 16 * 4 + k] = cw[2048:2064, k]
        convw[32:48, 16 * 4 + k] = cw[2064:2080, k]
    convb[0:16, 16] = cb[2048:2064]
    convb[32:48, 16] = cb[2064:2080]
    dtbias = np.zeros((128, 1), np.float32)
    dtbias[64:128, 0] = np.asarray(inp["L0_attn_dt_bias"])
    alog = np.zeros((128, 1), np.float32)
    alog[64:128, 0] = np.asarray(inp["L0_attn_A_log"])
    dvec = np.zeros((128, 16), np.float32)
    Dv = np.asarray(inp["L0_attn_D"])
    for k in range(16):
        dvec[:, k] = np.repeat(Dv[4 * k:4 * k + 4], 32)
    normw = np.zeros((128, 16), np.float32)
    nw = np.asarray(inp["L0_attn_norm_w"])
    for fc in range(16):
        normw[:, fc] = nw[fc * 128:(fc + 1) * 128]
    identb = np.eye(128, dtype=ml_dtypes.bfloat16)
    identf = np.eye(128, dtype=np.float32)
    maskt = np.where(np.arange(128)[None, :] >= np.arange(128)[:, None], 0.0, NEG).astype(ml_dtypes.bfloat16)
    onesb = np.ones((1, 128), ml_dtypes.bfloat16)
    masktf = np.where(np.arange(128)[None, :] >= np.arange(128)[:, None], 0.0, NEG).astype(np.float32)
    onesf = np.ones((128, 128), np.float32)
    onescolb = np.ones((128, 1), ml_dtypes.bfloat16)
    onesallb = np.ones((128, 128), ml_dtypes.bfloat16)
    mask4b = np.tile(np.where(np.arange(128)[None, :] >= np.arange(128)[:, None], 0.0, NEG), (1, 4)).astype(ml_dtypes.bfloat16)
    delta3 = np.zeros((4, 128), ml_dtypes.bfloat16)
    for gg in range(3):
        delta3[gg, 32 * gg:32 * gg + 16] = 1.0
    wgt = np.ascontiguousarray(np.asarray(inp["L0_ffn_w_gate"]).T).reshape(8, 128, 4096)
    wut = np.ascontiguousarray(np.asarray(inp["L0_ffn_w_up"]).T).reshape(8, 128, 4096)
    wdt = np.ascontiguousarray(np.asarray(inp["L0_ffn_w_down"]).T).reshape(32, 128, 1024)
    return {
        "u": np.ascontiguousarray(u_full[core * NSEQ:(core + 1) * NSEQ]),
        "wint": wint_t, "woutt": woutt, "convw": convw, "convb": convb,
        "dtbias": dtbias, "alog": alog, "dvec": dvec, "normw": normw,
        "identb": identb, "identf": identf, "maskt": maskt, "masktf": masktf, "onesf": onesf, "onesb": onesb,
        "delta3": delta3, "onescolb": onescolb, "onesallb": onesallb, "mask4b": mask4b, "wgt": wgt, "wut": wut, "wdt": wdt,
    }


_CACHED = {}


def kernel(**inputs):
    import concourse.bass_utils as _bu
    if "nc" not in _CACHED:
        nc, _taps = build(dbg=())
        _CACHED["nc"] = nc
    nc = _CACHED["nc"]
    in_maps = [host_inputs(inputs, core) for core in range(8)]
    res = _bu.run_bass_kernel_spmd(nc, in_maps, core_ids=list(range(8)))
    out = np.concatenate([np.asarray(r["out"]) for r in res.results], axis=0)
    return out.astype(np.float32)


# version 12
# speedup vs baseline: 1.0698x; 1.0698x over previous
import numpy as np
import ml_dtypes
from contextlib import ExitStack
import concourse.bass as bass
import concourse.mybir as mybir

F32 = mybir.dt.float32
BF16 = mybir.dt.bfloat16
Alu = mybir.AluOpType
Act = mybir.ActivationFunctionType
AX = mybir.AxisListType.X

M1 = 12582912.0
M2 = 6755399441055744.0
M3 = 3221225472.0

NSEQ = 2
L = 2048
Q = 128
GRP = 512
NGRP = L // GRP
NCH = GRP // Q
H = 64
DST = 16
DM = 1024
DI = 2048
CONV = 2080
EPS = 1e-5
NEG = -1e9

ENGINES = ("sync", "scalar", "vector", "tensor", "gpsimd")


QSEMS = {"sync": ("tr", "st"), "scalar": ("kld", "ys", "tr2"), "gpsimd": ("gld", "gs")}


class Ctx:
    def __init__(self, nc, stack):
        self.nc = nc
        self.stack = stack
        self.ops = {k: [] for k in ENGINES}
        self.sems = {}
        self.cnt = {}
        self._cls = {"sync": None, "scalar": None}

    def _switch_cls(self, eng, cls):
        if cls != self._cls[eng] and self._cls[eng] is not None:
            for s in QSEMS[eng]:
                self.wait(eng, s)
        self._cls[eng] = cls

    def sem(self, name):
        if name not in self.sems:
            self.sems[name] = self.stack.enter_context(self.nc.semaphore(name))
            self.cnt[name] = 0
        return self.sems[name]

    def emit(self, eng, fn):
        self.ops[eng].append(fn)

    def op(self, eng, fn, inc=None, inck=1):
        if inc is not None:
            s = self.sem(inc)
            self.cnt[inc] += inck
            self.emit(eng, lambda e, fn=fn, s=s, k=inck: fn(e).then_inc(s, k))
        else:
            self.emit(eng, lambda e, fn=fn: fn(e))

    def dma(self, eng, out, in_, inc, s2s=False, cast=True, **kw):
        if eng in ("sync", "scalar"):
            self._switch_cls(eng, "other")
        if eng == "gpsimd" and cast:
            self.wait("gpsimd", "tr")
            self.wait("gpsimd", "tr2")
        s = self.sem(inc)
        self.cnt[inc] += 16
        self.emit(eng, lambda e, out=out, in_=in_, s=s, kw=kw: e.dma_start(out, in_, **kw).then_inc(s, 16))

    def dmat(self, eng, out, in_, inc=None):
        inc = inc or ("tr" if eng == "sync" else "tr2")
        self._switch_cls(eng, "tr")
        self.wait(eng, "gld")
        s = self.sem(inc)
        self.cnt[inc] += 16
        self.emit(eng, lambda e, out=out, in_=in_, s=s: e.dma_start_transpose(out, in_).then_inc(s, 16))

    def wait(self, eng, sem, val=None):
        s = self.sem(sem)
        v = self.cnt[sem] if val is None else val
        if v > 0:
            self.emit(eng, lambda e, s=s, v=v: e.wait_ge(s, v))

    def mark(self, sem):
        return (sem, self.cnt[sem])

    def wait_mark(self, eng, mark):
        if mark is None:
            return
        sem, v = mark
        self.wait(eng, sem, v)

    def barrier(self):
        for eng in ENGINES:
            for sname in list(self.cnt):
                self.wait(eng, sname)

    def finalize(self):
        nc = self.nc
        with nc.Block() as block:
            @block.sync
            def _(e):
                for f in self.ops["sync"]:
                    f(e)

            @block.scalar
            def _(e):
                for f in self.ops["scalar"]:
                    f(e)

            @block.vector
            def _(e):
                for f in self.ops["vector"]:
                    f(e)

            @block.tensor
            def _(e):
                for f in self.ops["tensor"]:
                    f(e)

            @block.gpsimd
            def _(e):
                for f in self.ops["gpsimd"]:
                    f(e)


def build(dbg=(), phases=("ssd", "wout", "B"), ssd_parts=("stage", "g", "r", "states", "ey", "gate", "rms")):
    nc = bass.Bass("TRN2", target_bir_lowering=False, detect_race_conditions=False)
    stack = ExitStack()
    cx = Ctx(nc, stack)
    dbg = set(dbg)

    dr = {}
    def din(name, shape, dt=F32):
        dr[name] = nc.dram_tensor(name, list(shape), dt, kind="ExternalInput").ap()
        return dr[name]

    u = din("u", [NSEQ, L, DM])
    din("wint", [33, 8, 128, 128])
    din("woutt", [2048, 1024])
    din("convw", [128, 17 * 4])
    din("convb", [128, 17])
    din("dtbias", [128, 1])
    din("alog", [128, 1])
    din("dvec", [128, 16])
    din("normw", [128, 16])
    din("identb", [128, 128], BF16)
    din("identf", [128, 128], F32)
    din("maskt", [128, 128], BF16)
    din("masktf", [128, 128], F32)
    din("onesf", [128, 128], F32)
    din("onesb", [1, 128], BF16)
    din("delta3", [4, 128], BF16)
    din("onescolb", [128, 1], BF16)
    din("onesallb", [128, 128], BF16)
    din("mask4b", [128, 512], BF16)
    din("wgt", [8, 128, 4096])
    din("wut", [8, 128, 4096])
    din("wdt", [32, 128, 1024])

    out = nc.dram_tensor("out", [NSEQ, L, DM], F32, kind="ExternalOutput").ap()
    outacc = nc.dram_tensor("outacc", [NSEQ, L, DM], F32, kind="Internal").ap()
    xq2scr = nc.dram_tensor("xq2scr", [32, 128, 32 * 128], BF16, kind="Internal").ap()
    gscr = nc.dram_tensor("gscr", [32, 128, 4096], BF16, kind="Internal").ap()
    xq1scr = nc.dram_tensor("xq1scr", [32, 128, 8 * 128], BF16, kind="Internal").ap()

    taps = {}
    def tap(name, shape, dt=F32):
        if name in dbg:
            taps[name] = nc.dram_tensor("tap_" + name, list(shape), dt, kind="ExternalOutput").ap()
            return taps[name]
        return None

    ARENA_SZ = 212800
    backing = stack.enter_context(nc.sbuf_tensor("arena", [128, ARENA_SZ], mybir.dt.uint8))
    backing_addr = nc.lookup_mloc(backing).addr
    zone_off = {"P": 0, "A": 40960, "B": 40960}
    zone_cap = {"P": 40960, "A": ARENA_SZ, "B": ARENA_SZ}

    def sb(name, shape, dt, zone="A"):
        import functools, operator
        per_part = functools.reduce(operator.mul, list(shape)[1:], 1) * mybir.dt.size(dt)
        addr = zone_off[zone]
        zone_off[zone] = (addr + per_part + 31) // 32 * 32
        assert zone_off[zone] <= zone_cap[zone], f"zone {zone} overflow at {name}: {zone_off[zone]}"
        h = nc.virtual_sbuf_tensor(name, list(shape), dt)
        nc._tensor(h.name, h.shape, h.dtype, type="SB")
        mloc = nc.lookup_mloc(h)
        mloc.allocated = True
        mloc.addr = backing_addr + addr
        return h

    def sb_alias(name, shape, dt, base_tensor, byte_off=0):
        h = nc.virtual_sbuf_tensor(name, list(shape), dt)
        nc._tensor(h.name, h.shape, h.dtype, type="SB")
        mloc = nc.lookup_mloc(h)
        mloc.allocated = True
        mloc.addr = nc.lookup_mloc(base_tensor).addr + byte_off
        return h

    def pt(name, shape, dt=F32):
        return stack.enter_context(nc.psum_tensor(name, list(shape), dt))

    c_identb = sb("c_identb", [128, 128], BF16, zone="P")
    c_identf = sb("c_identf", [128, 128], F32, zone="P")
    c_maskt = sb("c_maskt", [128, 128], BF16, zone="P")
    c_masktf = sb("c_masktf", [128, 128], F32, zone="P")
    c_onesf = sb("c_onesf", [128, 128], F32, zone="P")
    c_ones = sb("c_ones", [1, 128], BF16, zone="P")
    c_delta3 = sb("c_delta3", [4, 128], BF16, zone="P")
    c_onescolb = sb("c_onescolb", [128, 1], BF16, zone="P")
    c_onesallb = sb("c_onesallb", [128, 128], BF16, zone="P")
    c_mask4 = sb("c_mask4", [128, 512], BF16, zone="P")
    c_convw = sb("c_convw", [128, 68], F32, zone="P")
    c_convb = sb("c_convb", [128, 17], F32, zone="P")
    c_dtb = sb("c_dtb", [128, 1], F32, zone="P")
    c_A = sb("c_A", [128, 1], F32, zone="P")
    c_dvec = sb("c_dvec", [128, 16], F32, zone="P")
    c_normw = sb("c_normw", [128, 16], F32, zone="P")
    convdiag = sb("convdiag", [128, 8 * 128], BF16)

    winbuf = sb("winbuf", [128, 2 * 8 * 128], BF16)
    wobuf = sb("wobuf", [128, 2 * 512], BF16, zone="P")

    uc = sb("uc", [128, 2 * DM], F32, zone="P")
    ures = sb("ures", [128, DM], F32, zone="P")
    xnb = sb("xnb", [128, 2 * DM], BF16)
    xnF = sb("xnF", [128, 8 * GRP], BF16)
    stat = sb("stat", [128, 16], F32, zone="P")
    sq = sb("sq", [128, DM], BF16, zone="P")
    siluzF = sb("siluzF", [128, 16 * GRP], BF16)
    xbcF = sb("xbcF", [128, 17 * (GRP + 4)], BF16)
    convxF = sb("convxF", [128, 16 * GRP], BF16)
    bct = sb("bct", [48, GRP], BF16)
    ctile0 = sb("ctile0", [16, GRP], BF16)
    dtF = sb("dtF", [128, GRP], F32)
    acum = sb("acum", [128, GRP], F32)
    decayF = sb("decayF", [128, Q], F32, zone="P")
    dtT = sb("dtT", [128, 64], BF16, zone="P")
    decayT = sb("decayT", [128, 64], BF16, zone="P")
    negAcumT = sb("negAcumT", [128, 2 * 64], F32, zone="P")
    aflatH = sb("aflatH", [128, 2 * 22 * Q], BF16)
    aflatL = sb("aflatL", [128, 2 * 22 * Q], BF16)
    aflatL2 = sb("aflatL2", [128, 2 * 22 * Q], BF16)
    ahib = sb("ahib", [128, Q], BF16)
    alob = sb("alob", [128, Q], BF16)
    alo2b = sb("alo2b", [128, Q], BF16)
    tA = sb("tA", [128, Q], F32, zone="P")
    expacumF = sb("expacumF", [128, Q], BF16)
    decendcol = sb("decendcol", [128, 1], F32)
    decrow = sb("decrow", [1, 64], F32)
    expR3 = sb("expR3", [4, 22 * Q], BF16)
    ccrep = sb("ccrep", [128, Q], BF16)
    x_T = sb("x_T", [128, DI], BF16)
    xdt_T = sb("xdt_T", [128, 2 * DI], BF16, zone="P")
    xdd_T = sb("xdd_T", [128, DI], BF16)
    bt_tok = sb("bt_tok", [128, 16], BF16)
    etile = sb_alias("etile", [128, 4 * 128], BF16, convdiag, 0)
    emtile = sb_alias("emtile", [128, 4 * 128], BF16, convdiag, 1024)
    stile = sb("stile", [128, 128], BF16)
    r3 = sb("r3", [128, 22 * Q], BF16)
    prevR = sb("prevR", [128, 2 * 22 * 32], BF16, zone="P")
    prevb = sb("prevb", [16, DI], BF16)
    prev0 = sb("prev0", [16, DI], F32)
    decend = sb("decend", [16, 64], F32)
    yP = sb("yP", [32, 16 * Q], BF16)
    yF = sb("yF", [128, 16 * Q], BF16)
    yg = sb("yg", [128, 16 * Q], BF16)
    ygw = sb("ygw", [128, 16 * GRP], BF16)
    invrg = sb("invrg", [128, NCH], F32, zone="P")
    abar = sb_alias("abar", [128, GRP], F32, yg)
    ygsq = sb_alias("ygsq", [128, 16 * Q], BF16, xnb)
    rmsrow = sb("rmsrow", [1, Q], F32, zone="P")
    att = sb("att", [128, DM], F32, zone="P")

    xnt = sb("xnt", [128, DM], F32, zone="B")
    qv = sb("qv", [128, 2048], F32, zone="B")
    qa = sb("qa", [128, 2048], F32, zone="B")
    xq1t = sb("xq1t", [128, DM], BF16, zone="B")
    xq1F = sb("xq1F", [128, 2 * 8 * 128], BF16, zone="B")
    gateT = sb("gateT", [128, 2 * 4096], BF16, zone="B")
    hpT = sb("hpT", [128, 4096], F32, zone="B")
    xq2t = sb("xq2t", [128, 4096], BF16, zone="B")
    stripb = sb("stripb", [128, 2 * 4096], BF16, zone="B")
    wslot = sb("wslot", [128, 32768], BF16, zone="B")
    wtmp = sb("wtmp", [128, 4096], F32, zone="B")
    qb = wtmp.ap()[:, 0:2048]
    qc = wtmp.ap()[:, 2048:4096]
    mwc = sb("mwc", [128, 8], F32, zone="P")
    mws = sb("mws", [128, 4], F32, zone="P")
    mwrow = sb("mwrow", [1, 2], F32, zone="P")
    vblend = sb("vblend", [128, 1], F32, zone="P")

    pb = [pt("pb%d" % i, [128, GRP]) for i in range(8)]
    ps_win = [pb[0], pb[1]]
    ps_conv = [pb[2], pb[3]]

    for name, t in [("identb", c_identb), ("identf", c_identf), ("maskt", c_maskt), ("masktf", c_masktf), ("onesf", c_onesf),
                    ("onesb", c_ones), ("delta3", c_delta3), ("onescolb", c_onescolb), ("onesallb", c_onesallb), ("mask4b", c_mask4), ("convw", c_convw),
                    ("convb", c_convb), ("dtbias", c_dtb), ("alog", c_A),
                    ("dvec", c_dvec), ("normw", c_normw)]:
        cx.dma("scalar", t.ap(), dr[name], "kld")
    cx.wait("vector", "kld")
    cx.wait("scalar", "kld")
    cx.op("scalar", lambda e: e.activation(c_A[64:128, :], c_A[64:128, :], Act.Exp), inc="sc")
    cx.wait("vector", "sc")
    cx.op("vector", lambda e: e.tensor_scalar_mul(c_A[64:128, :], c_A[64:128, :], -1.0), inc="vc")
    cx.op("vector", lambda e: e.memset(vblend[:, :], 1.0), inc="vc")
    cx.wait("vector", "vc")
    cx.op("vector", lambda e: e.tensor_scalar(vblend[:, :], vblend[:, :], 9.094947017729282e-13, None, Alu.add), inc="vc")
    cx.wait("vector", "vc")
    cx.op("vector", lambda e: e.tensor_scalar(vblend[:, :], vblend[:, :], 1.0, 1099511627776.0, Alu.subtract, Alu.mult), inc="vc")
    cx.wait("vector", "vc")
    cx.op("vector", lambda e: e.tensor_scalar(vblend[:, :], vblend[:, :], -1.0, 1.0, Alu.mult, Alu.add), inc="vc")
    cx.wait("vector", "vc")
    cx.op("vector", lambda e: e.memset(expR3[0:4, :], 0.0), inc="vc")
    cx.op("vector", lambda e: e.memset(ccrep[:, :], 0.0), inc="vc")
    cx.op("vector", lambda e: e.memset(prevR[:, :], 0.0), inc="vc")
    cx.op("vector", lambda e: e.memset(xbcF[:, 16 * (GRP + 4): 17 * (GRP + 4)], 0.0), inc="vc")
    cx.barrier()

    tap_xn = tap("xn", [NSEQ, L, DM], BF16)
    tap_xbcpre = tap("xbcpre", [NSEQ, L, CONV], BF16)
    tap_siluz = tap("siluz", [NSEQ, L, DI], BF16)
    tap_dt = tap("dt", [NSEQ, H, L])
    tap_acum = tap("acum", [NSEQ, 16, H, Q])
    tap_xbc = tap("xbc", [NSEQ, L, CONV], BF16)
    tap_y = tap("y", [NSEQ, L, DI], BF16)
    tap_yg = tap("yg", [NSEQ, L, DI], BF16)
    tap_att = tap("att", [NSEQ, L, DM])

    def phase_a_group(s, g):
        base_tok = g * GRP
        xnb_tr = []
        for c in range(NCH):
            ucol = uc[:, (c % 2) * DM:(c % 2 + 1) * DM]
            cx.dma("scalar", ucol, u[s, base_tok + c * Q: base_tok + (c + 1) * Q, :], "kld")
            uld = cx.mark("kld")
            cx.wait_mark("scalar", uld)
            cx.op("scalar", lambda e, ucol=ucol: e.activation(
                sq[:, :], ucol, Act.Square, accum_out=stat[:, 0:1]), inc="sc")
            sq_m = cx.mark("sc")
            cx.wait_mark("vector", uld)
            cx.op("vector", lambda e, ucol=ucol: e.reduce_sum(stat[:, 8:9], ucol, axis=AX), inc="vc")
            cx.wait_mark("vector", sq_m)
            cx.wait("vector", "vc")
            cx.op("vector", lambda e: e.tensor_scalar_mul(stat[:, 2:3], stat[:, 8:9], 1.0 / DM), inc="vc")
            cx.op("vector", lambda e: e.tensor_scalar_mul(stat[:, 3:4], stat[:, 8:9], -1.0 / DM), inc="vc")
            cx.op("vector", lambda e: e.tensor_scalar_mul(stat[:, 4:5], stat[:, 0:1], 1.0 / DM), inc="vc")
            cx.wait("vector", "vc")
            cx.op("vector", lambda e: e.scalar_tensor_tensor(
                stat[:, 5:6], stat[:, 2:3], stat[:, 2:3], stat[:, 4:5], Alu.mult, Alu.subtract), inc="vc")
            cx.wait("vector", "vc")
            cx.op("vector", lambda e: e.tensor_scalar(stat[:, 6:7], stat[:, 5:6], -1.0, EPS, Alu.mult, Alu.add), inc="vc")
            cx.wait("vector", "vc")
            cx.op("vector", lambda e: e.reciprocal(stat[:, 7:8], stat[:, 6:7]), inc="vc")
            rec_m = cx.mark("vc")
            cx.wait_mark("scalar", rec_m)
            cx.op("scalar", lambda e: e.activation(stat[:, 9:10], stat[:, 7:8], Act.Sqrt), inc="sc")
            sqrt_m = cx.mark("sc")
            cx.wait_mark("vector", sqrt_m)
            if c >= 2:
                cx.wait_mark("vector", xnb_tr[c - 2])
            cx.op("vector", lambda e, ucol=ucol, c=c: e.tensor_scalar(
                xnb[:, (c % 2) * DM:(c % 2 + 1) * DM], ucol, stat[:, 3:4], stat[:, 9:10], Alu.add, Alu.mult), inc="vc")
            stat_free = cx.mark("vc")
            cx.wait_mark("scalar", stat_free)
            cx.wait_mark("sync", stat_free)
            for fc in range(8):
                cx.dmat("sync", xnF[:, fc * GRP + c * Q: fc * GRP + (c + 1) * Q],
                        xnb[:, (c % 2) * DM + fc * 128: (c % 2) * DM + (fc + 1) * 128], "tr")
            if tap_xn is not None:
                cx.dma("sync", tap_xn[s, base_tok + c * Q:base_tok + (c + 1) * Q, :],
                       xnb[:, (c % 2) * DM:(c % 2 + 1) * DM], "st")
            xnb_tr.append(cx.mark("tr"))
        xnF_done = cx.mark("tr")

        cx.wait_mark("tensor", xnF_done)
        evac_marks = []
        for oc in range(33):
            buf = oc % 2
            wslice = winbuf[:, buf * 1024:(buf + 1) * 1024]
            if oc >= 2:
                cx.wait("gpsimd", "tc")
            cx.dma("gpsimd", wslice.rearrange("f (c o) -> f c o", c=8), dr["wint"][oc].rearrange("c f o -> f c o"), "gld")
            cx.wait("tensor", "gld")
            psb = ps_win[oc % 2]
            if oc >= 2:
                cx.wait_mark("tensor", evac_marks[oc - 2])
            for fc in range(8):
                cx.op("tensor", lambda e, fc=fc, psb=psb, wslice=wslice: e.matmul(
                    psb[:, :], wslice[:, fc * 128:(fc + 1) * 128], xnF[:, fc * GRP:(fc + 1) * GRP],
                    start=(fc == 0), stop=(fc == 7)), inc="tc" if fc == 7 else None)
            mmark = cx.mark("tc")
            if oc < 16:
                cx.wait_mark("scalar", mmark)
                cx.op("scalar", lambda e, oc=oc, psb=psb: e.activation(
                    siluzF[:, oc * GRP:(oc + 1) * GRP], psb[:, :], Act.Silu), inc="sc")
                evac_marks.append(cx.mark("sc"))
            elif oc < 32:
                xi = oc - 16
                cx.wait_mark("vector", mmark)
                cx.op("vector", lambda e, xi=xi, psb=psb: e.tensor_copy(
                    xbcF[:, xi * (GRP + 4) + 3: xi * (GRP + 4) + 3 + GRP], psb[:, :]), inc="vc")
                evac_marks.append(cx.mark("vc"))
            else:
                cx.wait_mark("vector", mmark)
                cx.op("vector", lambda e, psb=psb: e.tensor_copy(
                    xbcF[0:16, 16 * (GRP + 4) + 3: 16 * (GRP + 4) + 3 + GRP], psb[0:16, :]), inc="vc")
                cx.op("vector", lambda e, psb=psb: e.tensor_copy(
                    xbcF[32:48, 16 * (GRP + 4) + 3: 16 * (GRP + 4) + 3 + GRP], psb[32:48, :]), inc="vc")
                cx.wait_mark("scalar", mmark)
                cx.op("scalar", lambda e, psb=psb: e.activation(
                    dtF[64:128, :], psb[64:128, :], Act.Exp, bias=c_dtb[64:128, :]), inc="sc")
                cx.wait("scalar", "sc")
                cx.op("scalar", lambda e: e.activation(
                    dtF[64:128, :], dtF[64:128, :], Act.Ln, bias=1.0), inc="sc")
                evac_marks.append(cx.mark("vc"))
        win_done_v = cx.mark("vc")
        win_done_s = cx.mark("sc")
        if tap_xbcpre is not None:
            cx.wait_mark("sync", win_done_v)
            for ci in range(16):
                cx.dma("sync", tap_xbcpre[s, base_tok:base_tok + GRP, ci * 128:(ci + 1) * 128].rearrange("t f -> f t"),
                       xbcF[:, ci * (GRP + 4) + 3: ci * (GRP + 4) + 3 + GRP], "st", allow_slow_non_contiguous=True)
            cx.dma("sync", tap_xbcpre[s, base_tok:base_tok + GRP, 2048:2064].rearrange("t f -> f t"),
                   xbcF[0:16, 16 * (GRP + 4) + 3: 16 * (GRP + 4) + 3 + GRP], "st", allow_slow_non_contiguous=True)
            cx.dma("sync", tap_xbcpre[s, base_tok:base_tok + GRP, 2064:2080].rearrange("t f -> f t"),
                   xbcF[32:48, 16 * (GRP + 4) + 3: 16 * (GRP + 4) + 3 + GRP], "st", allow_slow_non_contiguous=True)
        if tap_siluz is not None:
            cx.wait_mark("sync", win_done_s)
            for ci in range(16):
                cx.dma("sync", tap_siluz[s, base_tok:base_tok + GRP, ci * 128:(ci + 1) * 128].rearrange("t f -> f t"),
                       siluzF[:, ci * GRP:(ci + 1) * GRP], "st", allow_slow_non_contiguous=True)
        cx.wait_mark("vector", win_done_s)
        cx.op("vector", lambda e: e.tensor_scalar_mul(abar[64:128, :], dtF[64:128, :], c_A[64:128, 0:1]), inc="vc")
        cx.wait("vector", "vc")
        for c in range(NCH):
            cx.op("vector", lambda e, c=c: e.tensor_tensor_scan(
                acum[64:128, c * Q:(c + 1) * Q], abar[64:128, c * Q:(c + 1) * Q],
                abar[64:128, c * Q:(c + 1) * Q], 0.0, Alu.add, Alu.bypass), inc="vc")
        dt_done = cx.mark("vc")
        if tap_dt is not None:
            cx.wait_mark("sync", dt_done)
            cx.dma("sync", tap_dt[s, :, base_tok:base_tok + GRP], dtF[64:128, :], "st")
        if tap_acum is not None:
            cx.wait_mark("sync", dt_done)
            for c in range(NCH):
                cx.dma("sync", tap_acum[s, g * NCH + c, :, :], acum[64:128, c * Q:(c + 1) * Q], "st")

        cx.wait_mark("tensor", win_done_v)
        cx.wait_mark("tensor", win_done_s)
        conv_marks = []
        diag_marks = []
        for ci in range(17):
            psb = ps_conv[ci % 2]
            slot = (ci % 2) * 4
            if ci >= 2:
                cx.wait_mark("vector", conv_marks[ci - 2])
            for k in range(4):
                cx.op("vector", lambda e, ci=ci, k=k, slot=slot: e.tensor_scalar_mul(
                    convdiag[:, (slot + k) * 128:(slot + k + 1) * 128], c_identb[:, :],
                    c_convw[:, (ci * 4 + k):(ci * 4 + k) + 1]), inc="vc")
            diag_marks.append(cx.mark("vc"))
            cx.wait_mark("tensor", diag_marks[ci])
            if ci >= 2:
                cx.wait_mark("tensor", conv_marks[ci - 2])
            for k in range(4):
                cx.op("tensor", lambda e, ci=ci, k=k, slot=slot, psb=psb: e.matmul(
                    psb[:, :], convdiag[:, (slot + k) * 128:(slot + k + 1) * 128],
                    xbcF[:, ci * (GRP + 4) + k: ci * (GRP + 4) + k + GRP],
                    start=(k == 0), stop=(k == 3)), inc="tc" if k == 3 else None)
            cmark = cx.mark("tc")
            cx.wait_mark("scalar", cmark)
            if ci < 16:
                cx.op("scalar", lambda e, ci=ci, psb=psb: e.activation(
                    convxF[:, ci * GRP:(ci + 1) * GRP], psb[:, :], Act.Silu,
                    bias=c_convb[:, ci:ci + 1]), inc="sc")
            else:
                cx.op("scalar", lambda e, psb=psb: e.activation(
                    bct[0:16, :], psb[0:16, :], Act.Silu, bias=c_convb[0:16, 16:17]), inc="sc")
                cx.op("scalar", lambda e, psb=psb: e.activation(
                    bct[32:48, :], psb[32:48, :], Act.Silu, bias=c_convb[32:48, 16:17]), inc="sc")
            conv_marks.append(cx.mark("sc"))
        conv_done = cx.mark("sc")
        cx.wait_mark("gpsimd", conv_done)
        cx.dma("gpsimd", ctile0[0:16, :], bct[32:48, :], "gs", cast=False)
        ct_m = cx.mark("gs")
        cx.wait_mark("vector", conv_done)
        cx.wait("vector", "vc")
        for ci in range(17):
            cx.op("vector", lambda e, ci=ci: e.tensor_copy(
                xbcF[:, ci * (GRP + 4): ci * (GRP + 4) + 3],
                xbcF[:, ci * (GRP + 4) + GRP: ci * (GRP + 4) + GRP + 3]), inc="vc")
        if tap_xbc is not None:
            cx.wait_mark("sync", conv_done)
            for ci in range(16):
                cx.dma("sync", tap_xbc[s, base_tok:base_tok + GRP, ci * 128:(ci + 1) * 128].rearrange("t f -> f t"),
                       convxF[:, ci * GRP:(ci + 1) * GRP], "st", allow_slow_non_contiguous=True)
            cx.dma("sync", tap_xbc[s, base_tok:base_tok + GRP, 2048:2064].rearrange("t f -> f t"), bct[0:16, :], "st", allow_slow_non_contiguous=True)
            cx.dma("sync", tap_xbc[s, base_tok:base_tok + GRP, 2064:2080].rearrange("t f -> f t"), bct[32:48, :], "st", allow_slow_non_contiguous=True)
        return conv_done, ct_m, conv_marks

    pipe = {
        "x_T_free": None, "bt_free": None, "expR3_free": None, "ccrep_free": None,
        "decrow_free": None, "decendcol_free": None, "expacum_free": None,
        "split_free": None, "rec_done": None, "y_mm_last": None, "e_v_last": None,
        "e_s_last": None, "negA_free": [None, None], "aflat_free": [None, None],
        "prevR_free": [None, None], "xdt_free": [None, None],
        "ybank": [None, None, None, None], "ydma_last": None, "yf_read": None,
        "ygsq_done": None, "pb4_free": None, "pb7_rms_read": None,
        "stile_read": None, "r3_read": None, "xdd_free": None,
        "rmsrow_free": None, "pending_rms": None, "rms_mm": None,
    }

    def emit_rms():
        if pipe["pending_rms"] is None:
            return
        ygsq_m, cidx = pipe["pending_rms"]
        pipe["pending_rms"] = None
        cx.wait_mark("tensor", ygsq_m)
        for ci in range(16):
            cx.op("tensor", lambda e, ci=ci: e.matmul(
                pb[7][0:1, 384:512], c_onescolb[:, 0:1], ygsq[:, ci * 128:(ci + 1) * 128],
                start=(ci == 0), stop=(ci == 15)), inc="tc" if ci == 15 else None)
        rms_m = cx.mark("tc")
        pipe["rms_mm"] = rms_m
        cx.wait_mark("vector", rms_m)
        cx.wait_mark("vector", pipe["rmsrow_free"])
        cx.op("vector", lambda e: e.tensor_scalar(
            rmsrow[0:1, :], pb[7][0:1, 384:512], 1.0 / DI, EPS, Alu.mult, Alu.add), inc="vc")
        pipe["pb7_rms_read"] = cx.mark("vc")
        cx.wait("vector", "vc")
        cx.op("vector", lambda e: e.reciprocal(rmsrow[0:1, :], rmsrow[0:1, :]), inc="vc")
        rr_m = cx.mark("vc")
        cx.wait_mark("scalar", rr_m)
        cx.op("scalar", lambda e: e.activation(rmsrow[0:1, :], rmsrow[0:1, :], Act.Sqrt), inc="sc")
        rs_m = cx.mark("sc")
        cx.wait_mark("scalar", rs_m)
        cx.dma("scalar", invrg[:, cidx:cidx + 1], rmsrow[0:1, :], "ys", s2s=True)
        pipe["rmsrow_free"] = cx.mark("ys")

    def ssd_chunk(s, g, c, dt_done, conv_done, ct_m, conv_marks, parts=()):
        cq = c * Q
        par = c % 2
        base_tok = g * GRP + cq
        ach = acum[64:128, cq:cq + Q]
        cx.wait_mark("vector", dt_done)
        cx.op("vector", lambda e: e.tensor_scalar(
            decayF[64:128, :], ach, -1.0, acum[64:128, cq + Q - 1:cq + Q], Alu.mult, Alu.add), inc="vc")
        d0_m = cx.mark("vc")
        cx.wait_mark("scalar", d0_m)
        cx.op("scalar", lambda e: e.activation(decayF[64:128, :], decayF[64:128, :], Act.Exp), inc="sc")
        cx.wait_mark("scalar", dt_done)
        cx.wait_mark("scalar", pipe["expacum_free"])
        cx.op("scalar", lambda e: e.activation(expacumF[64:128, :], ach, Act.Exp), inc="sc")
        cx.wait_mark("scalar", pipe["decendcol_free"])
        cx.op("scalar", lambda e: e.activation(decendcol[64:128, :], acum[64:128, cq + Q - 1:cq + Q], Act.Exp), inc="sc")
        dec_m = cx.mark("sc")
        cx.wait_mark("tensor", dt_done)
        cx.wait_mark("tensor", dec_m)
        cx.wait_mark("tensor", pipe["pb4_free"])
        cx.op("tensor", lambda e: e.transpose(pb[4][:, 0:64], ach, c_identf[64:128, 64:128]), inc="tc")
        cx.op("tensor", lambda e: e.transpose(pb[4][:, 64:128], dtF[64:128, cq:cq + Q], c_identf[64:128, 64:128]), inc="tc")
        cx.op("tensor", lambda e: e.transpose(pb[4][:, 128:192], decayF[64:128, :], c_identf[64:128, 64:128]), inc="tc")
        tr_m = cx.mark("tc")
        cx.wait_mark("vector", tr_m)
        cx.wait_mark("vector", pipe["negA_free"][par])
        cx.op("vector", lambda e, par=par: e.tensor_scalar_mul(
            negAcumT[:, par * 64:(par + 1) * 64], pb[4][:, 0:64], -1.0), inc="vc")
        cx.op("vector", lambda e: e.tensor_copy(dtT[:, :], pb[4][:, 64:128]), inc="vc")
        cx.op("vector", lambda e: e.tensor_copy(decayT[:, :], pb[4][:, 128:192]), inc="vc")
        trc_m = cx.mark("vc")
        cx.wait_mark("vector", dt_done)
        cx.wait_mark("vector", pipe["split_free"])
        cx.op("vector", lambda e: e.tensor_copy(ahib[64:128, :], ach), inc="vc")
        cx.wait("vector", "vc")
        cx.op("vector", lambda e: e.scalar_tensor_tensor(
            tA[64:128, :], ahib[64:128, :], -1.0, ach, Alu.mult, Alu.add), inc="vc")
        cx.wait("vector", "vc")
        cx.op("vector", lambda e: e.tensor_copy(alob[64:128, :], tA[64:128, :]), inc="vc")
        cx.wait("vector", "vc")
        cx.op("vector", lambda e: e.scalar_tensor_tensor(
            alo2b[64:128, :], alob[64:128, :], -1.0, tA[64:128, :], Alu.mult, Alu.add), inc="vc")
        split_m = cx.mark("vc")
        cx.wait_mark("sync", pipe["x_T_free"])
        for ci in range(16):
            cx.wait_mark("sync", conv_marks[ci])
            cx.dmat("sync", x_T[:, ci * 128:(ci + 1) * 128],
                    convxF[:, ci * GRP + cq: ci * GRP + cq + Q], "tr")
        xT_m = cx.mark("tr")
        cx.wait_mark("sync", conv_done)
        cx.wait_mark("sync", pipe["bt_free"])
        cx.dmat("sync", bt_tok[:, 0:16], bct[0:16, cq:cq + Q], "tr")
        bt_m = cx.mark("tr")
        cx.wait_mark("gpsimd", split_m)
        cx.wait_mark("gpsimd", pipe["aflat_free"][par])
        for r in range(3):
            cnt = (22, 22, 20)[r]
            for fl, tile in ((aflatH, ahib), (aflatL, alob), (aflatL2, alo2b)):
                cx.dma("gpsimd", fl[32 * r:32 * r + 1, par * 22 * Q:par * 22 * Q + cnt * Q],
                       tile[64 + 22 * r:64 + 22 * r + cnt, :], "gs", cast=False)
        af_m = cx.mark("gs")
        cx.wait_mark("gpsimd", dec_m)
        cx.wait_mark("gpsimd", pipe["expR3_free"])
        for r in range(3):
            cnt = (22, 22, 20)[r]
            cx.dma("gpsimd", expR3[r:r + 1, 0:cnt * Q],
                   expacumF[64 + 22 * r:64 + 22 * r + cnt, :], "gs", cast=False)
        ex_m = cx.mark("gs")
        cx.wait_mark("gpsimd", ct_m)
        cx.wait_mark("gpsimd", pipe["ccrep_free"])
        for r in range(3):
            cx.dma("gpsimd", ccrep[32 * r:32 * r + 16, :], ctile0[0:16, cq:cq + Q], "gs", cast=False)
        cc_m = cx.mark("gs")
        cx.wait_mark("gpsimd", pipe["rec_done"])
        cx.wait_mark("gpsimd", pipe["prevR_free"][par])
        for r in range(3):
            cnt = (22, 22, 20)[r]
            cx.dma("gpsimd", prevR[32 * r:32 * r + 16, par * 22 * 32:par * 22 * 32 + cnt * 32],
                   prevb[0:16, 22 * r * 32:(22 * r + cnt) * 32], "gs", cast=False)
        pr_m = cx.mark("gs")
        cx.wait_mark("gpsimd", dec_m)
        cx.wait_mark("gpsimd", pipe["decrow_free"])
        cx.dma("gpsimd", decrow[0:1, :], decendcol[64:128, 0:1], "gs", cast=False)
        dcr_m = cx.mark("gs")
        pipe["expacum_free"] = ex_m
        pipe["decendcol_free"] = dcr_m
        pipe["split_free"] = af_m
        cx.wait_mark("tensor", dcr_m)
        cx.wait_mark("tensor", trc_m)
        cx.op("tensor", lambda e: e.matmul(pb[4][0:16, 320:384], c_onesf[0:1, 0:16],
                                           decrow[0:1, :], start=True, stop=True), inc="tc")
        dce_m = cx.mark("tc")
        pipe["decrow_free"] = dce_m
        cx.wait_mark("vector", dce_m)
        cx.op("vector", lambda e: e.tensor_copy(decend[0:16, :], pb[4][0:16, 320:384]), inc="vc")
        dcp_m = cx.mark("vc")
        pipe["pb4_free"] = dcp_m
        cx.wait_mark("vector", xT_m)
        cx.wait_mark("vector", pipe["xdt_free"][par])
        cx.op("vector", lambda e, par=par: e.tensor_tensor(
            xdt_T.ap()[:, par * DI:(par + 1) * DI].rearrange("p (h w) -> p h w", w=32),
            x_T.ap().rearrange("p (h w) -> p h w", w=32),
            dtT.ap().unsqueeze(2).broadcast_to([128, 64, 32]), Alu.mult), inc="vc")
        pipe["x_T_free"] = cx.mark("vc")
        cx.wait("vector", "vc")
        cx.wait_mark("vector", pipe["xdd_free"])
        cx.op("vector", lambda e, par=par: e.tensor_tensor(
            xdd_T.ap().rearrange("p (h w) -> p h w", w=32),
            xdt_T.ap()[:, par * DI:(par + 1) * DI].rearrange("p (h w) -> p h w", w=32),
            decayT.ap().unsqueeze(2).broadcast_to([128, 64, 32]), Alu.mult), inc="vc")
        xdd_m = cx.mark("vc")
        cx.wait_mark("tensor", ct_m)
        cx.wait_mark("tensor", conv_done)
        cx.wait_mark("tensor", pipe["stile_read"])
        cx.op("tensor", lambda e: e.matmul(pb[5][:, 0:128], bct[0:16, cq:cq + Q],
                                           ctile0[0:16, cq:cq + Q], start=True, stop=True), inc="tc")
        g_m = cx.mark("tc")
        cx.wait_mark("vector", g_m)
        cx.op("vector", lambda e: e.tensor_copy(stile[:, :], pb[5][:, 0:128]), inc="vc")
        st_m = cx.mark("vc")
        pipe["stile_read"] = st_m
        cx.wait_mark("tensor", ex_m)
        cx.wait_mark("tensor", cc_m)
        cx.wait_mark("tensor", st_m)
        cx.wait_mark("tensor", pipe["r3_read"])
        cx.wait_mark("tensor", pipe["e_s_last"])
        r_marks = {}
        rmm = None
        cx.wait_mark("vector", cc_m)
        for rg in range(22):
            bk = pb[5 + (rg % 2)]
            if rg >= 2:
                cx.wait_mark("tensor", r_marks[rg - 2])
            cx.op("tensor", lambda e, rg=rg, bk=bk: e.matmul(
                bk[:, 0:128], c_delta3[0:4, :],
                expR3[0:4, rg * Q:(rg + 1) * Q], start=True, stop=True), inc="tc")
            rmm = cx.mark("tc")
            cx.wait_mark("vector", rmm)
            cx.op("vector", lambda e, rg=rg, bk=bk: e.tensor_tensor(
                r3[:, rg * Q:(rg + 1) * Q], bk[:, 0:128],
                ccrep[:, :], Alu.mult), inc="vc")
            r_marks[rg] = cx.mark("vc")
        r_done = r_marks[21]
        pipe["expR3_free"] = rmm
        pipe["ccrep_free"] = r_done
        pipe["r3_read"] = r_done
        cx.wait_mark("tensor", xdd_m)
        cx.wait_mark("tensor", bt_m)
        st_mm = []
        for j in range(4):
            cx.wait_mark("tensor", pipe["ybank"][j])
            cx.op("tensor", lambda e, j=j: e.matmul(
                pb[j][0:16, :], bt_tok[:, 0:16], xdd_T[:, j * GRP:(j + 1) * GRP],
                start=True, stop=True), inc="tc")
            st_mm.append(cx.mark("tc"))
        pipe["bt_free"] = st_mm[3]
        pipe["xdd_free"] = st_mm[3]
        cx.wait_mark("vector", dcp_m)
        for j in range(4):
            cx.op("vector", lambda e, j=j: e.tensor_tensor(
                prev0.ap().rearrange("p (h w) -> p h w", w=32)[0:16, j * 16:(j + 1) * 16, :],
                prev0.ap().rearrange("p (h w) -> p h w", w=32)[0:16, j * 16:(j + 1) * 16, :],
                decend.ap().unsqueeze(2).broadcast_to([16, 64, 32])[0:16, j * 16:(j + 1) * 16, :], Alu.mult), inc="vc")
        cx.wait("vector", "vc")
        for j in range(4):
            cx.wait_mark("vector", st_mm[j])
            cx.op("vector", lambda e, j=j: e.tensor_tensor(
                prev0[0:16, j * GRP:(j + 1) * GRP], prev0[0:16, j * GRP:(j + 1) * GRP],
                pb[j][0:16, :], Alu.add), inc="vc")
        cx.wait("vector", "vc")
        cx.wait_mark("vector", pr_m)
        cx.op("vector", lambda e: e.tensor_copy(prevb[0:16, :], prev0[0:16, :]), inc="vc")
        rec_done = cx.mark("vc")
        pipe["rec_done"] = rec_done
        emit_rms()
        yhalf_marks = []
        e_vmarks = {}
        e_smarks = {}
        y_mm_marks = {}
        emm_last = None
        y_bank_free = [rec_done] * 4
        cx.wait_mark("tensor", r_done)
        cx.wait_mark("tensor", af_m)
        cx.wait_mark("tensor", pr_m)
        for half in range(2):
            for w in range(2):
                wave = half * 2 + w
                for b in range(4):
                    grp = wave * 4 + b
                    ep = pb[6 + (grp % 2)]
                    if grp == 1:
                        cx.wait_mark("tensor", pipe["pb7_rms_read"])
                    if (grp - 2) in e_vmarks:
                        cx.wait_mark("tensor", e_vmarks[grp - 2])
                        cx.wait_mark("tensor", e_smarks[grp - 2])
                    h0 = grp * 4
                    segs = []
                    j = 0
                    while j < 4:
                        h = h0 + j
                        b3 = 32 * (h // 22)
                        k = 1
                        while j + k < 4 and (h0 + j + k) // 22 == h // 22:
                            k += 1
                        segs.append((j, k, b3, h % 22))
                        j += k
                    for (j, k, b3, blk) in segs:
                        cx.op("tensor", lambda e, ep=ep, j=j, k=k: e.matmul(
                            ep[:, j * 128:(j + k) * 128], c_identb[:, :],
                            c_mask4[:, j * 128:(j + k) * 128], start=True, stop=False), inc=None)
                        for fi, fl in enumerate((aflatH, aflatL, aflatL2)):
                            last = (fi == 2) and (j + k == 4)
                            cx.op("tensor", lambda e, ep=ep, j=j, k=k, b3=b3, blk=blk, fl=fl, fi=fi, par=par: e.matmul(
                                ep[:, j * 128:(j + k) * 128], c_onesallb[b3:b3 + 1, 0:128],
                                fl[b3:b3 + 1, par * 22 * Q + blk * Q:par * 22 * Q + (blk + k) * Q],
                                start=False, stop=(fi == 2)), inc="tc" if last else None)
                    emm = cx.mark("tc")
                    emm_last = emm
                    cx.wait_mark("scalar", emm)
                    if grp == 0:
                        cx.wait_mark("scalar", trc_m)
                        cx.wait_mark("scalar", pipe["e_v_last"])
                    if (grp - 1) in e_vmarks:
                        cx.wait_mark("scalar", e_vmarks[grp - 1])
                    for j in range(4):
                        h = grp * 4 + j
                        cx.op("scalar", lambda e, ep=ep, j=j, h=h, par=par: e.activation(
                            etile[:, j * 128:(j + 1) * 128], ep[:, j * 128:(j + 1) * 128],
                            Act.Exp, bias=negAcumT[:, par * 64 + h:par * 64 + h + 1]), inc="sc")
                    e_smarks[grp] = cx.mark("sc")
                    cx.wait_mark("vector", e_smarks[grp])
                    cx.wait_mark("vector", st_m)
                    if (grp - 1) in y_mm_marks:
                        cx.wait_mark("vector", y_mm_marks[grp - 1])
                    elif grp == 0:
                        cx.wait_mark("vector", pipe["y_mm_last"])
                    cx.op("vector", lambda e: e.tensor_tensor(
                        emtile.ap().rearrange("p (j l) -> p j l", l=Q), etile.ap().rearrange("p (j l) -> p j l", l=Q),
                        stile.ap().unsqueeze(1).broadcast_to([128, 4, 128]), Alu.mult), inc="vc")
                    e_vmarks[grp] = cx.mark("vc")
                    cx.wait_mark("tensor", e_vmarks[grp])
                    cx.wait_mark("tensor", y_bank_free[b])
                    for j in range(4):
                        h = grp * 4 + j
                        b3 = 32 * (h // 22)
                        blk = h % 22
                        cx.op("tensor", lambda e, b=b, j=j, h=h, par=par: e.matmul(
                            pb[b][0:32, j * 128:(j + 1) * 128], xdt_T[:, par * DI + h * 32:par * DI + (h + 1) * 32],
                            emtile[:, j * 128:(j + 1) * 128], start=(j == 0), stop=False), inc=None)
                        last = (j == 3)
                        cx.op("tensor", lambda e, b=b, j=j, b3=b3, blk=blk, last=last, par=par: e.matmul(
                            pb[b][0:32, j * 128:(j + 1) * 128],
                            prevR[b3:b3 + 16, par * 22 * 32 + blk * 32:par * 22 * 32 + blk * 32 + 32],
                            r3[b3:b3 + 16, blk * Q:(blk + 1) * Q],
                            start=False, stop=last), inc="tc" if last else None)
                    ymm = cx.mark("tc")
                    y_mm_marks[grp] = ymm
                    cx.wait_mark("vector", ymm)
                    if yhalf_marks:
                        cx.wait_mark("vector", yhalf_marks[-1])
                    elif pipe["ydma_last"] is not None:
                        cx.wait_mark("vector", pipe["ydma_last"])
                    hh0 = b * 4
                    cx.op("vector", lambda e, b=b, hh0=hh0: e.tensor_copy(
                        yP[0:32, hh0 * 128:(hh0 + 4) * 128], pb[b][0:32, 0:512]), inc="vc")
                    y_bank_free[b] = cx.mark("vc")
                yh = cx.mark("vc")
                cx.wait_mark("scalar", yh)
                if wave == 0:
                    cx.wait_mark("scalar", pipe["yf_read"])
                for h4 in range(4):
                    cx.dma("scalar", yF[32 * h4:32 * h4 + 32, wave * 4 * Q:(wave * 4 + 4) * Q],
                           yP[0:32, :].rearrange("p (h16 h4p l) -> p h16 h4p l", h4p=4, l=Q)[:, :, h4, :],
                           "ys", s2s=True)
                yhalf_marks.append(cx.mark("ys"))
        y_done = cx.mark("ys")
        pipe["ydma_last"] = y_done
        pipe["y_mm_last"] = y_mm_marks[15]
        pipe["e_v_last"] = e_vmarks[15]
        pipe["e_s_last"] = e_smarks[15]
        pipe["negA_free"][par] = e_smarks[15]
        pipe["aflat_free"][par] = emm_last
        pipe["xdt_free"][par] = y_mm_marks[15]
        pipe["prevR_free"][par] = y_mm_marks[15]
        pipe["ybank"] = list(y_bank_free)
        cx.wait_mark("vector", y_done)
        cx.wait_mark("vector", conv_done)
        cx.wait_mark("vector", pipe["ygsq_done"])
        for ci in range(16):
            cx.op("vector", lambda e, ci=ci: e.scalar_tensor_tensor(
                yg[:, ci * 128:(ci + 1) * 128], convxF[:, ci * GRP + cq: ci * GRP + cq + Q],
                c_dvec[:, ci:ci + 1], yF[:, ci * 128:(ci + 1) * 128], Alu.mult, Alu.add), inc="vc")
        pipe["yf_read"] = cx.mark("vc")
        cx.wait("vector", "vc")
        for ci in range(16):
            cx.op("vector", lambda e, ci=ci: e.tensor_tensor(
                yg[:, ci * 128:(ci + 1) * 128], yg[:, ci * 128:(ci + 1) * 128],
                siluzF[:, ci * GRP + cq: ci * GRP + cq + Q], Alu.mult), inc="vc")
        yg_m = cx.mark("vc")
        cx.wait("vector", "vc")
        for ci in range(16):
            cx.op("vector", lambda e, ci=ci, c=c: e.tensor_scalar_mul(
                ygw[:, ci * GRP + cq: ci * GRP + cq + Q], yg[:, ci * 128:(ci + 1) * 128],
                c_normw[:, ci:ci + 1]), inc="vc")
        ygw_m = cx.mark("vc")
        cx.wait_mark("scalar", yg_m)
        cx.wait_mark("scalar", pipe["rms_mm"])
        for ci in range(16):
            cx.op("scalar", lambda e, ci=ci: e.activation(
                ygsq[:, ci * 128:(ci + 1) * 128], yg[:, ci * 128:(ci + 1) * 128], Act.Square), inc="sc")
        ygsq_m = cx.mark("sc")
        pipe["ygsq_done"] = ygsq_m
        pipe["pending_rms"] = (ygsq_m, c)
        if tap_y is not None:
            cx.wait_mark("sync", y_done)
            for ci in range(16):
                cx.dma("sync", tap_y[s, base_tok:base_tok + Q, ci * 128:(ci + 1) * 128].rearrange("t f -> f t"),
                       yF[:, ci * 128:(ci + 1) * 128], "st", allow_slow_non_contiguous=True)
        if tap_yg is not None:
            cx.wait_mark("sync", yg_m)
            for ci in range(16):
                cx.dma("sync", tap_yg[s, base_tok:base_tok + Q, ci * 128:(ci + 1) * 128].rearrange("t f -> f t"),
                       yg[:, ci * 128:(ci + 1) * 128], "st", allow_slow_non_contiguous=True)
        return ygw_m

    def wout_group(s, g, ygw_m):
        emit_rms()
        base_tok = g * GRP
        cx.wait_mark("tensor", ygw_m)
        for cb in range(NCH):
            cx.wait_mark("tensor", pipe["ybank"][cb])
        inv_gate = pipe["rmsrow_free"]
        ev_marks = []
        for dh in range(2):
            for f in range(16):
                buf = (dh * 16 + f) % 2
                wsl = wobuf[:, buf * 512:(buf + 1) * 512]
                if dh * 16 + f >= 2:
                    cx.wait("gpsimd", "tc")
                cx.dma("gpsimd", wsl, dr["woutt"][f * 128:(f + 1) * 128, dh * 512:(dh + 1) * 512], "gld")
                cx.wait("tensor", "gld")
                if f == 0 and len(ev_marks) >= 1:
                    for m in ev_marks[-NCH:]:
                        cx.wait_mark("tensor", m)
                for c in range(NCH):
                    cx.op("tensor", lambda e, c=c, f=f, wsl=wsl: e.matmul(
                        pb[c][:, :], ygw[:, f * GRP + c * Q: f * GRP + (c + 1) * Q], wsl,
                        start=(f == 0), stop=(f == 15)), inc="tc" if c == NCH - 1 else None)
            mm_m = cx.mark("tc")
            for c in range(NCH):
                cx.wait("scalar", "vc")
                cx.dma("scalar", ures[:, dh * 512:(dh + 1) * 512],
                       u[s, base_tok + c * Q:base_tok + (c + 1) * Q, dh * 512:(dh + 1) * 512], "kld")
                uld = cx.mark("kld")
                cx.wait_mark("vector", uld)
                cx.wait_mark("vector", mm_m)
                cx.wait_mark("vector", inv_gate)
                cx.wait("vector", "st")
                cx.op("vector", lambda e, c=c, dh=dh: e.scalar_tensor_tensor(
                    att[:, dh * 512:(dh + 1) * 512], pb[c][:, :], invrg[:, c:c + 1],
                    ures[:, dh * 512:(dh + 1) * 512], Alu.mult, Alu.add), inc="vc")
                evm = cx.mark("vc")
                ev_marks.append(evm)
                cx.wait_mark("sync", evm)
                cx.dma("sync", outacc[s, base_tok + c * Q:base_tok + (c + 1) * Q, dh * 512:(dh + 1) * 512],
                       att[:, dh * 512:(dh + 1) * 512], "st")
                if tap_att is not None:
                    cx.dma("sync", tap_att[s, base_tok + c * Q:base_tok + (c + 1) * Q, dh * 512:(dh + 1) * 512],
                           att[:, dh * 512:(dh + 1) * 512], "st")
        return cx.mark("st")

    for s in range(NSEQ):
        cx.op("vector", lambda e: e.memset(prev0[:, :], 0.0), inc="vc")
        cx.op("vector", lambda e: e.memset(prevb[:, :], 0.0), inc="vc")
        for ci in range(17):
            cx.op("vector", lambda e, ci=ci: e.memset(xbcF[:, ci * (GRP + 4): ci * (GRP + 4) + 3], 0.0), inc="vc")
        for g in range(NGRP):
            conv_done, ct_m, conv_marks = phase_a_group(s, g)
            dt_done = cx.mark("vc")
            if "ssd" in phases:
                ygw_m = None
                for c in range(NCH):
                    ygw_m = ssd_chunk(s, g, c, dt_done, conv_done, ct_m, conv_marks, parts=ssd_parts)
                if "wout" in phases:
                    wout_group(s, g, ygw_m)
            cx.barrier()

    tap_xq1 = tap("xq1", [NSEQ, L, DM], BF16)
    tap_gate = tap("gate", [NSEQ, L, 4096], BF16)
    tap_hq = tap("hq", [NSEQ, L, 4096], BF16)

    def quant_weights(wdram, nt, C, mws_col):
        cx.op("vector", lambda e: e.memset(mwc[:, 0:1], 0.0), inc="vc")
        cx.wait("vector", "vc")
        for i in range(nt):
            cx.wait("scalar", "sc")
            cx.wait("scalar", "vc")
            cx.dma("scalar", wtmp[:, 0:C], wdram[i], "kld")
            wl = cx.mark("kld")
            cx.wait_mark("scalar", wl)
            cx.op("scalar", lambda e: e.activation(hpT[:, 0:C], wtmp[:, 0:C], Act.Abs,
                                                   accum_out=mwc[:, 1:2]), inc="sc")
            am = cx.mark("sc")
            cx.wait_mark("vector", am)
            cx.op("vector", lambda e: e.tensor_tensor(mwc[:, 0:1], mwc[:, 0:1], mwc[:, 1:2], Alu.add), inc="vc")
            cx.wait("vector", "vc")
        s_m = cx.mark("vc")
        cx.wait_mark("tensor", s_m)
        cx.op("tensor", lambda e: e.matmul(pb[0][0:1, 0:1], mwc[:, 0:1], c_onesf[:, 0:1],
                                           start=True, stop=True), inc="tc")
        t_m = cx.mark("tc")
        cx.wait_mark("vector", t_m)
        cx.op("vector", lambda e, nt=nt, C=C: e.tensor_scalar(
            mwrow[0:1, 0:1], pb[0][0:1, 0:1], 1.0 / (nt * 128.0 * C), None, Alu.mult), inc="vc")
        cx.wait("vector", "vc")
        cx.op("vector", lambda e: e.tensor_scalar_max(mwrow[0:1, 0:1], mwrow[0:1, 0:1], 1e-5), inc="vc")
        m_m = cx.mark("vc")
        cx.wait_mark("tensor", m_m)
        cx.op("tensor", lambda e: e.matmul(pb[0][:, 1:2], c_onesf[0:1, 0:128], mwrow[0:1, 0:1],
                                           start=True, stop=True), inc="tc")
        b_m = cx.mark("tc")
        cx.wait_mark("vector", b_m)
        cx.op("vector", lambda e: e.tensor_scalar_mul(mwc[:, 4:5], pb[0][:, 1:2], 0.5), inc="vc")
        cx.op("vector", lambda e: e.tensor_scalar_mul(mwc[:, 5:6], pb[0][:, 1:2], -0.5), inc="vc")
        cx.op("vector", lambda e, mws_col=mws_col: e.tensor_scalar_mul(
            mws[:, mws_col:mws_col + 1], pb[0][:, 1:2], -1.0), inc="vc")
        cx.wait("vector", "vc")
        for i in range(nt):
            cx.wait("scalar", "vc")
            cx.dma("scalar", wtmp[:, 0:C], wdram[i], "kld")
            wl = cx.mark("kld")
            cx.wait_mark("vector", wl)
            cx.op("vector", lambda e, C=C: e.tensor_single_scalar(
                qv[:, 0:C] if C <= 2048 else hpT[:, 0:C], wtmp[:, 0:C], mwc[:, 4:5], Alu.is_gt), inc="vc")
            cx.wait("vector", "vc")
            cx.op("vector", lambda e, i=i, C=C: e.scalar_tensor_tensor(
                wslot[:, i * C:(i + 1) * C], wtmp[:, 0:C], mwc[:, 5:6],
                qv[:, 0:C] if C <= 2048 else hpT[:, 0:C], Alu.is_lt, Alu.subtract), inc="vc")
            cx.wait("vector", "vc")
        cx.barrier()

    bst = {"uc_v": None, "uc_s": None, "stat01": None, "xq1t_free": None, "xq2t_free": None,
           "out_st": None}

    def ln_quant1(s, ti, par, xq1F_gate):
        ucol = uc[:, 0:DM]
        cx.wait_mark("scalar", bst["uc_v"])
        cx.wait_mark("scalar", bst["uc_s"])
        cx.dma("scalar", ucol, u[s, ti * 128:(ti + 1) * 128, :], "kld")
        uld = cx.mark("kld")
        cx.wait_mark("scalar", uld)
        cx.wait_mark("scalar", bst["stat01"])
        cx.op("scalar", lambda e: e.activation(sq[:, :], ucol, Act.Square, accum_out=stat[:, 0:1]), inc="sc")
        sq_m = cx.mark("sc")
        bst["uc_s"] = sq_m
        cx.wait_mark("vector", uld)
        cx.op("vector", lambda e: e.reduce_sum(stat[:, 8:9], ucol, axis=AX), inc="vc")
        cx.wait_mark("vector", sq_m)
        cx.wait("vector", "vc")
        cx.op("vector", lambda e: e.tensor_scalar_mul(stat[:, 2:3], stat[:, 8:9], 1.0 / DM), inc="vc")
        cx.op("vector", lambda e: e.tensor_scalar_mul(stat[:, 3:4], stat[:, 8:9], -1.0 / DM), inc="vc")
        cx.op("vector", lambda e: e.tensor_scalar_mul(stat[:, 4:5], stat[:, 0:1], 1.0 / DM), inc="vc")
        bst["stat01"] = cx.mark("vc")
        cx.wait("vector", "vc")
        cx.op("vector", lambda e: e.scalar_tensor_tensor(
            stat[:, 5:6], stat[:, 2:3], stat[:, 2:3], stat[:, 4:5], Alu.mult, Alu.subtract), inc="vc")
        cx.wait("vector", "vc")
        cx.op("vector", lambda e: e.tensor_scalar(stat[:, 6:7], stat[:, 5:6], -1.0, EPS, Alu.mult, Alu.add), inc="vc")
        cx.wait("vector", "vc")
        cx.op("vector", lambda e: e.reciprocal(stat[:, 7:8], stat[:, 6:7]), inc="vc")
        rec_m = cx.mark("vc")
        cx.wait_mark("scalar", rec_m)
        cx.op("scalar", lambda e: e.activation(stat[:, 9:10], stat[:, 7:8], Act.Sqrt), inc="sc")
        sqrt_m = cx.mark("sc")
        cx.wait_mark("vector", sqrt_m)
        cx.op("vector", lambda e: e.tensor_scalar(
            xnt[:, :], ucol, stat[:, 3:4], stat[:, 9:10], Alu.add, Alu.mult), inc="vc")
        bst["uc_v"] = cx.mark("vc")
        cx.wait("vector", "vc")
        cx.op("vector", lambda e: e.reduce_max(stat[:, 10:11], xnt[:, :], axis=AX,
                                               apply_absolute_value=True), inc="vc")
        cx.wait("vector", "vc")
        cx.op("vector", lambda e: e.tensor_scalar_max(stat[:, 11:12], stat[:, 10:11], 1e-5), inc="vc")
        cx.wait("vector", "vc")
        cx.op("vector", lambda e: e.reciprocal(stat[:, 12:13], stat[:, 11:12]), inc="vc")
        cx.wait("vector", "vc")
        cx.op("vector", lambda e: e.tensor_scalar_mul(stat[:, 13:14], stat[:, 12:13], 127.0), inc="vc")
        cx.op("vector", lambda e: e.tensor_scalar_mul(stat[:, 14:15], stat[:, 11:12], 1.0 / 127.0), inc="vc")
        cx.wait("vector", "vc")
        cx.op("vector", lambda e: e.tensor_scalar_mul(qv[:, 0:DM], xnt[:, :], stat[:, 13:14]), inc="vc")
        cx.wait("vector", "vc")
        cx.op("vector", lambda e: e.tensor_scalar(qa[:, 0:DM], qv[:, 0:DM], M1, M1, Alu.add, Alu.subtract), inc="vc")
        cx.op("vector", lambda e: e.tensor_scalar(qb[:, 0:DM], qv[:, 0:DM], M2, M2, Alu.add, Alu.subtract), inc="vc")
        cx.op("vector", lambda e: e.tensor_scalar(qc[:, 0:DM], qv[:, 0:DM], M3, M3, Alu.add, Alu.subtract), inc="vc")
        cx.wait("vector", "vc")
        cx.op("vector", lambda e: e.tensor_tensor(qa[:, 0:DM], qa[:, 0:DM], qc[:, 0:DM], Alu.subtract), inc="vc")
        cx.wait("vector", "vc")
        cx.op("vector", lambda e: e.scalar_tensor_tensor(
            qa[:, 0:DM], qa[:, 0:DM], vblend[:, 0:1], qb[:, 0:DM], Alu.mult, Alu.add), inc="vc")
        cx.wait("vector", "vc")
        cx.wait_mark("vector", bst["xq1t_free"])
        cx.op("vector", lambda e: e.tensor_scalar_mul(xq1t[:, :], qa[:, 0:DM], stat[:, 14:15]), inc="vc")
        q_m = cx.mark("vc")
        cx.wait_mark("sync", q_m)
        for gm in xq1F_gate:
            cx.wait_mark("sync", gm)
        for fc in range(8):
            cx.dmat("sync", xq1F[:, par * 1024 + fc * 128:par * 1024 + (fc + 1) * 128],
                    xq1t[:, fc * 128:(fc + 1) * 128], "tr")
        bst["xq1t_free"] = cx.mark("tr")
        if tap_xq1 is not None:
            cx.dma("sync", tap_xq1[s, ti * 128:(ti + 1) * 128, :], xq1t[:, :], "st")
        return cx.mark("tr")

    def phase_b1():
        quant_weights(dr["wgt"], 8, 4096, 0)
        evh = [cx.mark("sc"), cx.mark("sc")]
        g_st = [None, None]
        xq_st = [None, None]
        mmh = []
        for gt in range(32):
            s, ti = gt // 16, gt % 16
            par = gt % 2
            xf_m = ln_quant1(s, ti, par,
                             [mmh[-2] if len(mmh) >= 2 else None, xq_st[par]])
            cx.wait_mark("gpsimd", xf_m)
            cx.dma("gpsimd", xq1scr[gt], xq1F[:, par * 1024:(par + 1) * 1024], "gs", cast=False)
            xq_st[par] = cx.mark("gs")
            cx.wait_mark("tensor", xf_m)
            for hf in range(2):
                cx.wait_mark("tensor", evh[hf])
                for fc in range(8):
                    for ob in range(hf * 4, hf * 4 + 4):
                        cx.op("tensor", lambda e, fc=fc, ob=ob, par=par: e.matmul(
                            pb[ob][:, :], xq1F[:, par * 1024 + fc * 128:par * 1024 + (fc + 1) * 128],
                            wslot[:, fc * 4096 + ob * 512: fc * 4096 + (ob + 1) * 512],
                            start=(fc == 0), stop=(fc == 7)), inc="tc" if ob == hf * 4 + 3 else None)
                mm_m = cx.mark("tc")
                cx.wait_mark("scalar", mm_m)
                cx.wait_mark("scalar", g_st[par])
                for ob in range(hf * 4, hf * 4 + 4):
                    cx.op("scalar", lambda e, ob=ob, par=par: e.activation(
                        gateT[:, par * 4096 + ob * 512:par * 4096 + (ob + 1) * 512], pb[ob][:, :], Act.Silu,
                        scale=mws[:, 0:1]), inc="sc")
                evh[hf] = cx.mark("sc")
            mmh.append(mm_m)
            cx.wait_mark("gpsimd", evh[1])
            cx.dma("gpsimd", gscr[gt], gateT[:, par * 4096:(par + 1) * 4096], "gs", cast=False)
            g_st[par] = cx.mark("gs")
            if tap_gate is not None:
                cx.wait_mark("sync", evh[1])
                cx.dma("sync", tap_gate[s, ti * 128:(ti + 1) * 128, :], gateT[:, par * 4096:(par + 1) * 4096], "st")
        cx.barrier()

    def phase_b2():
        quant_weights(dr["wut"], 8, 4096, 1)
        evh = [cx.mark("vc"), cx.mark("vc")]
        mmh = []
        evt = []
        for gt in range(32):
            s, ti = gt // 16, gt % 16
            par = gt % 2
            cx.wait_mark("scalar", mmh[-2] if len(mmh) >= 2 else None)
            cx.dma("scalar", xq1F[:, par * 1024:(par + 1) * 1024], xq1scr[gt], "kld")
            cx.wait_mark("scalar", evt[-2] if len(evt) >= 2 else None)
            cx.dma("scalar", gateT[:, par * 4096:(par + 1) * 4096], gscr[gt], "kld")
            ld_m = cx.mark("kld")
            cx.wait_mark("tensor", ld_m)
            mm_m = None
            for hf in range(2):
                cx.wait_mark("tensor", evh[hf])
                for fc in range(8):
                    for ob in range(hf * 4, hf * 4 + 4):
                        cx.op("tensor", lambda e, fc=fc, ob=ob, par=par: e.matmul(
                            pb[ob][:, :], xq1F[:, par * 1024 + fc * 128:par * 1024 + (fc + 1) * 128],
                            wslot[:, fc * 4096 + ob * 512: fc * 4096 + (ob + 1) * 512],
                            start=(fc == 0), stop=(fc == 7)), inc="tc" if ob == hf * 4 + 3 else None)
                mm_m = cx.mark("tc")
                cx.wait_mark("vector", mm_m)
                cx.wait_mark("vector", ld_m)
                for ob in range(hf * 4, hf * 4 + 4):
                    cx.op("vector", lambda e, ob=ob, par=par: e.scalar_tensor_tensor(
                        hpT[:, ob * 512:(ob + 1) * 512], pb[ob][:, :], mws[:, 1:2],
                        gateT[:, par * 4096 + ob * 512:par * 4096 + (ob + 1) * 512], Alu.mult, Alu.mult), inc="vc")
                evh[hf] = cx.mark("vc")
            mmh.append(mm_m)
            evt.append(evh[1])
            ev_m = evh[1]
            cx.wait("vector", "vc")
            cx.op("vector", lambda e: e.reduce_max(stat[:, 10:11], hpT[:, :], axis=AX,
                                                   apply_absolute_value=True), inc="vc")
            cx.wait("vector", "vc")
            cx.op("vector", lambda e: e.tensor_scalar_max(stat[:, 11:12], stat[:, 10:11], 1e-5), inc="vc")
            cx.wait("vector", "vc")
            cx.op("vector", lambda e: e.reciprocal(stat[:, 12:13], stat[:, 11:12]), inc="vc")
            cx.wait("vector", "vc")
            cx.op("vector", lambda e: e.tensor_scalar_mul(stat[:, 13:14], stat[:, 12:13], 127.0), inc="vc")
            cx.op("vector", lambda e: e.tensor_scalar_mul(stat[:, 14:15], stat[:, 11:12], 1.0 / 127.0), inc="vc")
            cx.wait("vector", "vc")
            for hh in range(2):
                hsl = slice(hh * 2048, (hh + 1) * 2048)
                cx.op("vector", lambda e, hsl=hsl: e.tensor_scalar_mul(qv[:, :], hpT[:, hsl], stat[:, 13:14]), inc="vc")
                cx.wait("vector", "vc")
                cx.op("vector", lambda e: e.tensor_scalar(qa[:, :], qv[:, :], M1, M1, Alu.add, Alu.subtract), inc="vc")
                cx.op("vector", lambda e: e.tensor_scalar(qb[:, :], qv[:, :], M2, M2, Alu.add, Alu.subtract), inc="vc")
                cx.op("vector", lambda e: e.tensor_scalar(qc[:, :], qv[:, :], M3, M3, Alu.add, Alu.subtract), inc="vc")
                cx.wait("vector", "vc")
                cx.op("vector", lambda e: e.tensor_tensor(qa[:, :], qa[:, :], qc[:, :], Alu.subtract), inc="vc")
                cx.wait("vector", "vc")
                cx.op("vector", lambda e: e.scalar_tensor_tensor(
                    qa[:, :], qa[:, :], vblend[:, 0:1], qb[:, :], Alu.mult, Alu.add), inc="vc")
                cx.wait("vector", "vc")
                if hh == 0:
                    cx.wait_mark("vector", bst["xq2t_free"])
                cx.op("vector", lambda e, hsl=hsl: e.tensor_scalar_mul(xq2t[:, hsl], qa[:, :], stat[:, 14:15]), inc="vc")
                cx.wait("vector", "vc")
            q_m = cx.mark("vc")
            cx.wait_mark("sync", q_m)
            cx.dma("sync", xq2scr[gt], xq2t[:, :], "st")
            bst["xq2t_free"] = cx.mark("st")
            if tap_hq is not None:
                cx.dma("sync", tap_hq[s, ti * 128:(ti + 1) * 128, :], xq2t[:, :], "st")
        cx.barrier()

    def phase_c():
        quant_weights(dr["wdt"], 32, 1024, 2)
        evh = [cx.mark("vc"), cx.mark("vc")]
        mmh = []
        evt = []
        for gt in range(32):
            s, ti = gt // 16, gt % 16
            par = gt % 2
            cx.wait_mark("sync", mmh[-2] if len(mmh) >= 2 else None)
            for oc in range(32):
                cx.dmat("sync", stripb[:, par * 4096 + oc * 128:par * 4096 + (oc + 1) * 128],
                        xq2scr[gt][:, oc * 128:(oc + 1) * 128])
            tr_s = cx.mark("tr")
            tr_c = cx.mark("tr")
            cx.wait_mark("gpsimd", evt[-1] if evt else None)
            cx.dma("gpsimd", ures[:, :], outacc[s, ti * 128:(ti + 1) * 128, :], "gs", cast=False)
            ld_m = cx.mark("gs")
            cx.wait_mark("tensor", tr_s)
            cx.wait_mark("tensor", tr_c)
            mm_m = None
            for dh in range(2):
                cx.wait_mark("tensor", evh[dh])
                for oc in range(32):
                    cx.op("tensor", lambda e, oc=oc, dh=dh, par=par: e.matmul(
                        pb[dh][:, :], stripb[:, par * 4096 + oc * 128:par * 4096 + (oc + 1) * 128],
                        wslot[:, oc * 1024 + dh * 512: oc * 1024 + (dh + 1) * 512],
                        start=(oc == 0), stop=(oc == 31)), inc="tc" if oc == 31 else None)
                mm_m = cx.mark("tc")
                cx.wait_mark("vector", mm_m)
                cx.wait_mark("vector", ld_m)
                cx.wait_mark("vector", bst["out_st"])
                cx.op("vector", lambda e, dh=dh: e.scalar_tensor_tensor(
                    att[:, dh * 512:(dh + 1) * 512], pb[dh][:, :], mws[:, 2:3],
                    ures[:, dh * 512:(dh + 1) * 512], Alu.mult, Alu.add), inc="vc")
                evh[dh] = cx.mark("vc")
            mmh.append(mm_m)
            evt.append(evh[1])
            cx.wait_mark("gpsimd", evh[1])
            cx.dma("gpsimd", out[s, ti * 128:(ti + 1) * 128, :], att[:, :], "gs", cast=False)
            bst["out_st"] = cx.mark("gs")
        cx.barrier()

    if "B" in phases:
        phase_b1()
        phase_b2()
        phase_c()

    cx.wait("sync", "st")
    cx.wait("sync", "tr")
    cx.finalize()
    stack.close()
    return nc, taps


def host_inputs(inp, core):
    u_full = np.ascontiguousarray(inp["L0_attn_u"])
    win = np.asarray(inp["L0_attn_W_in"])
    wint = np.zeros((4224, 1024), np.float32)
    wint[:4096] = win[:4096]
    wint[4096:4112] = win[4096:4112]
    wint[4128:4144] = win[4112:4128]
    wint[4160:4224] = win[4128:4192]
    wint = np.ascontiguousarray(wint.T)
    wint_t = np.zeros((33, 8, 128, 128), np.float32)
    for oc in range(33):
        for fc in range(8):
            wint_t[oc, fc] = wint[fc * 128:(fc + 1) * 128, oc * 128:(oc + 1) * 128]
    woutt = np.ascontiguousarray(np.asarray(inp["L0_attn_W_out"]).T)
    cw = np.asarray(inp["L0_attn_conv_w"]).reshape(2080, 4)
    convw = np.zeros((128, 68), np.float32)
    convb = np.zeros((128, 17), np.float32)
    cb = np.asarray(inp["L0_attn_conv_b"])
    for ci in range(16):
        for k in range(4):
            convw[:, ci * 4 + k] = cw[ci * 128:(ci + 1) * 128, k]
        convb[:, ci] = cb[ci * 128:(ci + 1) * 128]
    for k in range(4):
        convw[0:16, 16 * 4 + k] = cw[2048:2064, k]
        convw[32:48, 16 * 4 + k] = cw[2064:2080, k]
    convb[0:16, 16] = cb[2048:2064]
    convb[32:48, 16] = cb[2064:2080]
    dtbias = np.zeros((128, 1), np.float32)
    dtbias[64:128, 0] = np.asarray(inp["L0_attn_dt_bias"])
    alog = np.zeros((128, 1), np.float32)
    alog[64:128, 0] = np.asarray(inp["L0_attn_A_log"])
    dvec = np.zeros((128, 16), np.float32)
    Dv = np.asarray(inp["L0_attn_D"])
    for k in range(16):
        dvec[:, k] = np.repeat(Dv[4 * k:4 * k + 4], 32)
    normw = np.zeros((128, 16), np.float32)
    nw = np.asarray(inp["L0_attn_norm_w"])
    for fc in range(16):
        normw[:, fc] = nw[fc * 128:(fc + 1) * 128]
    identb = np.eye(128, dtype=ml_dtypes.bfloat16)
    identf = np.eye(128, dtype=np.float32)
    maskt = np.where(np.arange(128)[None, :] >= np.arange(128)[:, None], 0.0, NEG).astype(ml_dtypes.bfloat16)
    onesb = np.ones((1, 128), ml_dtypes.bfloat16)
    masktf = np.where(np.arange(128)[None, :] >= np.arange(128)[:, None], 0.0, NEG).astype(np.float32)
    onesf = np.ones((128, 128), np.float32)
    onescolb = np.ones((128, 1), ml_dtypes.bfloat16)
    onesallb = np.ones((128, 128), ml_dtypes.bfloat16)
    mask4b = np.tile(np.where(np.arange(128)[None, :] >= np.arange(128)[:, None], 0.0, NEG), (1, 4)).astype(ml_dtypes.bfloat16)
    delta3 = np.zeros((4, 128), ml_dtypes.bfloat16)
    for gg in range(3):
        delta3[gg, 32 * gg:32 * gg + 16] = 1.0
    wgt = np.ascontiguousarray(np.asarray(inp["L0_ffn_w_gate"]).T).reshape(8, 128, 4096)
    wut = np.ascontiguousarray(np.asarray(inp["L0_ffn_w_up"]).T).reshape(8, 128, 4096)
    wdt = np.ascontiguousarray(np.asarray(inp["L0_ffn_w_down"]).T).reshape(32, 128, 1024)
    return {
        "u": np.ascontiguousarray(u_full[core * NSEQ:(core + 1) * NSEQ]),
        "wint": wint_t, "woutt": woutt, "convw": convw, "convb": convb,
        "dtbias": dtbias, "alog": alog, "dvec": dvec, "normw": normw,
        "identb": identb, "identf": identf, "maskt": maskt, "masktf": masktf, "onesf": onesf, "onesb": onesb,
        "delta3": delta3, "onescolb": onescolb, "onesallb": onesallb, "mask4b": mask4b, "wgt": wgt, "wut": wut, "wdt": wdt,
    }


_CACHED = {}


def kernel(**inputs):
    import concourse.bass_utils as _bu
    if "nc" not in _CACHED:
        nc, _taps = build(dbg=())
        _CACHED["nc"] = nc
    nc = _CACHED["nc"]
    in_maps = [host_inputs(inputs, core) for core in range(8)]
    res = _bu.run_bass_kernel_spmd(nc, in_maps, core_ids=list(range(8)))
    out = np.concatenate([np.asarray(r["out"]) for r in res.results], axis=0)
    return out.astype(np.float32)
